# version 12
# baseline (speedup 1.0000x reference)
import numpy as np
import concourse.bass as bass
import concourse.mybir as mybir
from concourse.bass_types import AP

from concourse.bass_utils import run_bass_kernel_spmd

F32 = mybir.dt.float32
BF16 = mybir.dt.bfloat16
AF = mybir.ActivationFunctionType
ALU = mybir.AluOpType
AX = mybir.AxisListType
I32 = mybir.dt.int32

_DEBUG = False

D = 1024
DCH = 8
T = 2048
B_LOC = 2
NT = 16
NQ = B_LOC * NT
PROJ = 3072
FF = 4096
NE = 8
TGRP = 1024
NGRP = 4
CAP = 384
NST = CAP // 128
KS = 128
EPS = 1e-5
TTOT = NQ * 128
_SBUF_BASE = 16512

KB = 1024
OFF = {}
_cur = 0
def _lay(name, size, at=None):
    global _cur
    if at is not None:
        _cur = at
    OFF[name] = _cur
    _cur += size

_lay("ident", 256)
_lay("rev", 256)
_lay("ones", 1024)
_lay("coef", 1024)
_lay("scr", 128)
_lay("scr2", 128)
_lay("statp", 64)
_lay("strows", 256)
_lay("rt8", 64)
_lay("gcol", 64)
_lay("bcol", 64)
_lay("rw", 128)
assert _cur <= 4 * KB
_lay("bo", 2 * KB, at=4 * KB)
_lay("bi", 6 * KB)
_lay("srow", 6 * KB)
_lay("srtmp", 1 * KB)
assert _cur <= 20 * KB
_lay("tmp16", 12 * KB, at=172 * KB)
_lay("x_in", 8 * KB, at=20 * KB)
_lay("xn32", 8 * KB)
_lay("xnbf", 4 * KB)
_lay("xnt", 4 * KB)
_lay("vtmp", 4 * KB)
_lay("xsq", 2 * KB)
_lay("accp", 8 * KB)
assert _cur <= 60 * KB
_lay("y1tm", 32 * KB, at=20 * KB)
_lay("wo", 16 * KB, at=20 * KB)
_lay("y2t", 4 * KB)
_lay("accin", 8 * KB)
_lay("accout", 8 * KB)
assert _cur <= 60 * KB
_lay("wp", 48 * KB, at=60 * KB)
_lay("toep", 32 * KB, at=60 * KB)
_lay("vrev", 32 * KB, at=108 * KB)
_lay("y2tm", 32 * KB, at=108 * KB)
_lay("x1tm", 32 * KB, at=140 * KB)
_lay("y1rev", 32 * KB, at=140 * KB)
_lay("x2tm", 32 * KB, at=172 * KB)
assert _cur <= 206 * KB
_lay("ltri", 512, at=4 * KB)
_lay("onesf", 512)
_lay("iot", 1536)
_lay("maskm", 256)
_lay("posd", 512)
_lay("tta", 768)
_lay("rowbuf", 1 * KB)
_lay("coefbf", 512)
_lay("gcoefc", 128)
_lay("xnbg", 16 * KB, at=10 * KB)
_lay("w1c", 16 * KB, at=26 * KB)
_lay("b2r", 4 * KB, at=42 * KB)
_lay("b1call", 1 * KB)
_lay("b1r", 8 * KB, at=188 * KB)
_lay("gt", 12 * KB, at=48 * KB)
_lay("gs", 12 * KB, at=60 * KB)
_lay("ysb", 12 * KB, at=72 * KB)
_lay("xgsb", 4 * KB, at=84 * KB)
_lay("xgt", 12 * KB, at=88 * KB)
_lay("hT", 24 * KB, at=100 * KB)
_lay("w2c", 32 * KB, at=124 * KB)
_lay("ygrp", 32 * KB, at=156 * KB)
assert _cur <= 206 * KB, _cur


class _Em:

    def __init__(self):
        self.marks = {}
        self.counts = {}
        self.passno = 1
        self.sems = {}
        self.cur_engine = None

    def start_pass(self, n):
        self.passno = n
        self.counts = {}

    def op(self, f, mark=None, sem=None, dma=False):
        if mark is not None:
            assert sem is not None, mark
            inc = 16 if dma else 1
            self.counts[sem] = self.counts.get(sem, 0) + inc
            if self.passno == 1:
                assert mark not in self.marks, f"dup mark {mark}"
                self.marks[mark] = (sem, self.counts[sem])
                return None
            instr = f()
            instr.then_inc(self.sems[sem], inc)
            return instr
        if self.passno == 1:
            return None
        return f()

    def wait(self, label):
        if self.passno == 1:
            assert isinstance(label, str)
            return
        semname, val = self.marks[label]
        self.cur_engine.wait_ge(self.sems[semname], val)

    def wait_total(self, semname):
        val = self.counts.get(semname, 0)
        if self.passno == 1:
            return
        self.cur_engine.wait_ge(self.sems[semname], val)


def _build_nc():
    nc = bass.Bass("TRN2", target_bir_lowering=False)

    dx = nc.dram_tensor("x", [TTOT, D], F32, kind="ExternalInput").ap()
    dwin = nc.dram_tensor("w_in", [D, PROJ], F32, kind="ExternalInput").ap()
    dbin = nc.dram_tensor("b_in", [PROJ], F32, kind="ExternalInput").ap()
    dk0 = nc.dram_tensor("k0", [D, KS], F32, kind="ExternalInput").ap()
    dk1 = nc.dram_tensor("k1", [D, KS], F32, kind="ExternalInput").ap()
    dgam = nc.dram_tensor("gamma", [D], F32, kind="ExternalInput").ap()
    dbet = nc.dram_tensor("beta", [D], F32, kind="ExternalInput").ap()
    dwout = nc.dram_tensor("w_out", [D, D], F32, kind="ExternalInput").ap()
    dbout = nc.dram_tensor("b_out", [D], F32, kind="ExternalInput").ap()
    drw = nc.dram_tensor("rw", [D, NE], F32, kind="ExternalInput").ap()
    dw1 = nc.dram_tensor("w1t", [NE, D, FF], F32, kind="ExternalInput").ap()
    db1 = nc.dram_tensor("b1", [NE, FF], F32, kind="ExternalInput").ap()
    dw2 = nc.dram_tensor("w2t", [NE, FF, D], F32, kind="ExternalInput").ap()
    db2 = nc.dram_tensor("b2", [NE, D], F32, kind="ExternalInput").ap()
    dout = nc.dram_tensor("out", [TTOT, D], F32, kind="ExternalOutput").ap()

    ikind = "ExternalOutput" if _DEBUG else "Internal"
    dacc = nc.dram_tensor("dbg_acc", [TTOT, D], F32, kind=ikind).ap()
    dxnb = nc.dram_tensor("dbg_xnb", [TTOT, D], BF16, kind=ikind).ap()
    dwall = nc.dram_tensor("wall", [D, 768], BF16, kind="Internal").ap()
    dtoep = nc.dram_tensor("dtoep", [128, 128 * 4096], BF16, kind="Internal").ap()
    dwob = nc.dram_tensor("wob", [D, D], BF16, kind="Internal").ap()
    dw1b = nc.dram_tensor("w1b", [NE, D, FF], BF16, kind="Internal").ap()
    dw2b = nc.dram_tensor("w2b", [NE, FF, D], BF16, kind="Internal").ap()
    if _DEBUG:
        dcoef = nc.dram_tensor("dbg_coef", [128, NQ * 8], F32, kind=ikind).ap()
        dvrev = nc.dram_tensor("dbg_vrev", [128, NT * D], BF16, kind=ikind).ap()
        dy2 = nc.dram_tensor("dbg_y2", [128, NT * D], BF16, kind=ikind).ap()

    arena = nc.alloc_sbuf_tensor_at(
        "arena", [128, 206 * KB // 2], BF16, offset=_SBUF_BASE).ap()

    S = {}
    def sba(key, shape, dt, off_name=None, extra=0):
        off_name = off_name or key
        off = OFF[off_name] + extra
        esz = 2 if dt == BF16 else 4
        nbf16 = shape[1] * esz // 2
        S[key] = arena[0:shape[0], off // 2: off // 2 + nbf16].bitcast(dt)

    sba("ident", [128, 128], BF16)
    sba("rev", [128, 128], BF16)
    sba("ones", [128, 512], BF16)
    sba("coef", [128, 256], F32)
    sba("scr", [128, 32], F32)
    sba("scr2", [128, 32], F32)
    sba("statp", [128, 2], BF16)
    sba("strows", [2, 128], BF16)
    sba("rt8", [128, 8], F32)
    sba("gcol", [128, 8], F32)
    sba("bcol", [128, 8], BF16)
    sba("rw", [128, 64], BF16)
    sba("bo", [1, 1024], BF16)
    sba("bi", [1, PROJ], BF16)
    sba("srow", [2, PROJ], BF16)
    sba("srtmp", [1, 512], BF16)
    for p in range(2):
        sba(f"tmp16_{p}", [128, PROJ], BF16, "tmp16", p * 6 * KB)
        sba(f"x_in{p}", [128, 1024], F32, "x_in", p * 4 * KB)
        sba(f"xn32{p}", [128, 1024], F32, "xn32", p * 4 * KB)
        sba(f"xnbf{p}", [128, 1024], BF16, "xnbf", p * 2 * KB)
        sba(f"xnt{p}", [128, 1024], BF16, "xnt", p * 2 * KB)
        sba(f"vtmp{p}", [128, 1024], BF16, "vtmp", p * 2 * KB)
        sba(f"accp{p}", [128, 1024], F32, "accp", p * 4 * KB)
        sba(f"y2t{p}", [128, 1024], BF16, "y2t", p * 2 * KB)
        sba(f"accin{p}", [128, 1024], F32, "accin", p * 4 * KB)
        sba(f"accout{p}", [128, 1024], F32, "accout", p * 4 * KB)
    sba("xsq", [128, 1024], BF16)
    sba("y1tm", [128, NT * D], BF16)
    sba("wo", [128, DCH * D], BF16)
    sba("wp", [128, DCH * PROJ], BF16)
    for p in range(4):
        sba(f"toep{p}", [128, 32 * 128], BF16, "toep", p * 8 * KB)
    sba("vrev", [128, NT * D], BF16)
    sba("y2tm", [128, NT * D], BF16)
    sba("x1tm", [128, NT * D], BF16)
    sba("y1rev", [128, NT * D], BF16)
    sba("x2tm", [128, NT * D], BF16)
    sba("b1r", [8, FF], BF16)
    for p in range(4):
        sba(f"w1c{p}", [128, DCH * 256], BF16, "w1c", p * 4 * KB)
    for p in range(2):
        sba(f"b2r{p}", [1, D], BF16, "b2r", p * 2 * KB)
    sba("b1call", [128, 256], F32)
    sba("ltri", [128, 128], F32)
    sba("onesf", [128, 128], F32)
    sba("iot", [128, 384], F32)
    sba("maskm", [128, 64], F32)
    for p in range(2):
        sba(f"posd{p}", [128, 64], F32, "posd", p * 256)
        sba(f"gcoefc{p}", [128, 3], F32, "gcoefc", p * 64)
        sba(f"gt{p}", [128, 8 * 384], BF16, "gt", p * 6 * KB)
        sba(f"gs{p}", [128, 3 * 1024], BF16, "gs", p * 6 * KB)
        sba(f"ysb{p}", [128, 3 * 1024], BF16, "ysb", p * 6 * KB)
        sba(f"xgsb{p}", [128, 1024], BF16, "xgsb", p * 2 * KB)
        sba(f"xgt{p}", [128, DCH * CAP], BF16, "xgt", p * 6 * KB)
    for nm in ("tt0", "tt1", "tt2"):
        sba(nm, [128, 64], F32, "tta", ("tt0", "tt1", "tt2").index(nm) * 256)
    for p in range(4):
        sba(f"row{p}", [1, 64], F32, "rowbuf", p * 256)
        sba(f"w2c{p}", [128, 4 * 1024], BF16, "w2c", p * 8 * KB)
    sba("coefbf", [128, 256], BF16)
    sba("xnbg", [128, 8 * 1024], BF16)
    sba("hT", [128, 32 * CAP], BF16)
    sba("ygrp", [128, 8 * 1024], F32)

    PB = [nc.alloc_psum_tensor(f"pb{i}", [128, 512], F32).ap() for i in range(7)]
    pb7 = nc.alloc_psum_tensor("pb7", [128, 1024], BF16).ap()
    pb7f = pb7[:].bitcast(F32)

    em = _Em()
    SEMNAMES = (["s0", "sxg", "syg", "ssr", "sout", "swo", "sb1", "sdb",
                "scv2", "stx0", "stx1", "scwo",
                "ste", "sve", "ssc", "sgp"]
                + [f"sx{p}" for p in range(2)] + [f"si{p}" for p in range(2)]
                + [f"sp{p}" for p in range(2)] + [f"sn{p}" for p in range(2)]
                + [f"sw{p}" for p in range(2)] + [f"swi{p}" for p in range(2)]
                + [f"stp{p}" for p in range(4)]
                + [f"sw1{p}" for p in range(4)] + [f"sw2{p}" for p in range(4)]
                + [f"scv1_{e}" for e in range(NE)]
                + [f"sb2{p}" for p in range(2)])

    GBANK = lambda sb: (PB[2 * (sb % 2)], PB[2 * (sb % 2) + 1])
    BCH = [2, 3, 4, 5, 6, 7]

    def wall_src(cv, chunk, cg):
        off = cv * 384 + (129 if chunk == 0 else 1) + (cg * 32) * 768
        return AP(dwall.tensor, off, [[1, 128], [768, 32], [1, 128]])

    def rearr(t):
        return t[:].rearrange("p (j d) -> p d j", d=1024)

    qlast = lambda b: b * NT + NT - 1

    def body_gpsimd(g):
        em.cur_engine = g
        em.op(lambda: g.memset(S["ident"][:], 0.0), mark="gp_m1", sem="sgp")
        em.wait("gp_m1")
        em.op(lambda: g.affine_select(out=S["ident"][:], in_=S["ident"][:],
                                      compare_op=ALU.not_equal, fill=1.0, base=0,
                                      pattern=[[-1, 128]], channel_multiplier=1),
              mark="gp_m2", sem="sgp")
        em.op(lambda: g.memset(S["rev"][:], 0.0), mark="gp_m3", sem="sgp")
        em.wait("gp_m3")
        em.op(lambda: g.affine_select(out=S["rev"][:], in_=S["rev"][:],
                                      compare_op=ALU.not_equal, fill=1.0, base=-127,
                                      pattern=[[1, 128]], channel_multiplier=1),
              mark="gp_const", sem="sgp")
        for cc in range(DCH):
            em.op(lambda cc=cc: g.dma_start(
                out=S["gcol"][:, cc:cc + 1], in_=AP(dgam.tensor, cc * 128, [[1, 128]])),
                mark=f"wd_gcol{cc}", sem="s0", dma=True)
            em.op(lambda cc=cc: g.dma_start(
                out=S["bcol"][:, cc:cc + 1], in_=AP(dbet.tensor, cc * 128, [[1, 128]])),
                mark=f"wd_bcol{cc}", sem="s0", dma=True)
        em.op(lambda: g.dma_start(
            out=S["rw"][:].rearrange("p (c e) -> p c e", c=DCH),
            in_=AP(drw.tensor, 0, [[8, 128], [1024, DCH], [1, 8]])),
            mark="wd_rw", sem="s0", dma=True)
        em.op(lambda: g.dma_start(
            out=S["bo"][0:1, :], in_=AP(dbout.tensor, 0, [[1024, 1], [1, 1024]])),
            mark="wd_bo", sem="s0", dma=True)
        em.op(lambda: g.dma_start(
            out=S["bi"][0:1, :], in_=AP(dbin.tensor, 0, [[PROJ, 1], [1, PROJ]])),
            mark="wd_bi", sem="s0", dma=True)
        em.wait("ve_zrow")
        for r in range(8):
            em.op(lambda r=r: g.dma_start(
                out=AP(dwall.tensor, r * 128 * 768, [[768, 128], [1, 768]]),
                in_=S["xsq"][:, 0:768]),
                mark=f"wd_wz{r}", sem="s0", dma=True)
        em.wait("wd_wz7")
        em.op(lambda: g.dma_start(
            out=AP(dwall.tensor, 128, [[768, D], [1, KS]]), in_=dk0),
            mark="wd_k0", sem="s0", dma=True)
        em.op(lambda: g.dma_start(
            out=AP(dwall.tensor, 384 + 128, [[768, D], [1, KS]]), in_=dk1),
            mark="wd_k1", sem="s0", dma=True)

        for b in range(B_LOC):
            for cc in range(DCH):
                if b == 1 and cc == 0:
                    em.wait("ve_g0_1_31")
                if cc >= 2:
                    em.wait(f"ve_wp{b}_{cc - 2}")
                    if b == 0:
                        em.wait(f"te_biasc{cc - 2}")
                em.op(lambda cc=cc: g.dma_start(
                    out=S[f"tmp16_{cc % 2}"][:],
                    in_=AP(dwin.tensor, cc * 128 * PROJ, [[PROJ, 128], [1, PROJ]])),
                    mark=f"wd_winc{b}_{cc}", sem=f"swi{cc % 2}", dma=True)
            if b == 0:
                em.op(lambda: g.dma_start(out=dwob, in_=dwout),
                      mark="cv_wo", sem="scwo", dma=True)
                for cv in range(2):
                    for cg in range(32):
                        for ch in range(2):
                            idx = (cv * 2 + ch) * 32 + cg
                            last = (cg == 31 and ch == 1)
                            em.op(lambda cv=cv, ch=ch, cg=cg, idx=idx: g.dma_start(
                                out=AP(dtoep.tensor, idx * 128 * 4096,
                                       [[4096, 128], [128, 32], [1, 128]]),
                                in_=wall_src(cv, ch, cg)),
                                mark=(f"gp_txe{cv}" if last else f"gp_tx{cv}_{ch}_{cg}"),
                                sem=f"stx{cv}", dma=True)
            for e in range(NE):
                if b == 0:
                    em.op(lambda e=e: g.dma_start(
                        out=AP(dw2b.tensor, e * FF * D, [[D, FF], [1, D]]),
                        in_=AP(dw2.tensor, e * FF * D, [[D, FF], [1, D]])),
                        mark=f"cv_w2_{e}", sem="scv2", dma=True)
                else:
                    em.op(lambda e=e: g.dma_start(
                        out=AP(dw1b.tensor, e * D * FF, [[FF, D], [1, FF]]),
                        in_=AP(dw1.tensor, e * D * FF, [[FF, D], [1, FF]])),
                        mark=f"cv_w1_{e}", sem=f"scv1_{e}", dma=True)



        em.wait("ve_y1_done")
        em.wait("sst_xnt30")
        em.wait("sst_xnt31")
        if _DEBUG:
            em.wait("sst_dbgy2")
        em.op(lambda: g.dma_start(out=S["b1r"][0:8, :], in_=db1),
              mark="sb1_all", sem="sb1", dma=True)
        for e0 in range(2):
            em.op(lambda e0=e0: g.dma_start(
                out=S[f"b2r{e0 % 2}"][0:1, :],
                in_=AP(db2.tensor, e0 * D, [[D, 1], [1, D]])),
                mark=f"wd_b2_0_{e0}", sem=f"sb2{e0 % 2}", dma=True)
        em.op(lambda: g.memset(S["ltri"][:], 0.0), mark="gp_p2a", sem="sgp")
        em.wait("gp_p2a")
        em.op(lambda: g.affine_select(out=S["ltri"][:], in_=S["ltri"][:],
                                      compare_op=ALU.is_ge, fill=1.0, base=0,
                                      pattern=[[-1, 128]], channel_multiplier=1))
        em.op(lambda: g.memset(S["onesf"][:], 1.0))
        em.op(lambda: g.iota(S["iot"][:], pattern=[[1, 384]], base=0,
                             channel_multiplier=0,
                             allow_small_or_imprecise_dtypes=True),
              mark="gp_p2c", sem="sgp")
        for gi in range(NGRP):
            for e in range(NE):
                ge = gi * NE + e
                if gi == 0:
                    em.wait(f"cv_w1_{e}")
                for pc in range(1, 16, 2):
                    if not (ge == 0 and pc < 4):
                        if pc >= 4:
                            em.wait(f"te_a{gi}_{e}_{pc * 2 - 7}")
                        else:
                            pe2, pg2 = (e - 1, gi) if e > 0 else (NE - 1, gi - 1)
                            em.wait(f"te_a{pg2}_{pe2}_{pc * 2 + 25}")
                    em.op(lambda e=e, pc=pc: g.dma_start(
                        out=S[f"w1c{pc % 4}"][:].rearrange("p (c f) -> p c f", c=DCH),
                        in_=AP(dw1b.tensor, e * D * FF + pc * 256,
                               [[FF, 128], [128 * FF, DCH], [1, 256]])),
                        mark=f"wd_w1_{gi}_{e}_{pc}", sem=f"sw1{pc % 4}", dma=True)
                for k in range(1, 8, 2):
                    if ge == 0 and k < 4:
                        em.wait("te_y2tr1_15")
                        em.wait_total("scv2")
                        if _DEBUG:
                            em.wait("sst_dbgy2")
                    elif k < 4:
                        pgi, pe = divmod(ge - 1, NE)
                        phx = 4 * k + 19
                        em.wait(f"te_bphi{pgi}_{pe}_{phx}" if phx < 31
                                else f"te_b{pgi}_{pe}_5")
                    else:
                        em.wait(f"te_bphi{gi}_{e}_{4 * (k - 4) + 3}")
                    em.op(lambda e=e, k=k: g.dma_start(
                        out=S[f"w2c{k % 4}"][:].rearrange("p (j c) -> p j c", j=4),
                        in_=AP(dw2b.tensor, e * FF * D + k * 4 * 128 * D,
                               [[D, 128], [128 * D, 4], [1, D]])),
                        mark=f"wd_w2_{gi}_{e}_{k}", sem=f"sw2{k % 4}", dma=True)
                ge2 = ge + 2
                if ge2 < NGRP * NE:
                    g2_, e2 = divmod(ge2, NE)
                    em.wait(f"te_bphi{gi}_{e}_0")
                    em.op(lambda e2=e2: g.dma_start(
                        out=S[f"b2r{e2 % 2}"][0:1, :],
                        in_=AP(db2.tensor, e2 * D, [[D, 1], [1, D]])),
                        mark=f"wd_b2_{g2_}_{e2}", sem=f"sb2{e2 % 2}", dma=True)

    def body_sync(sy):
        em.cur_engine = sy
        for ng in range(6):
            em.wait(f"ve_srow_bias{ng}")
            em.op(lambda ng=ng: sy.dma_start(
                out=S["srow"][1:2, ng * 512:(ng + 1) * 512], in_=S["srtmp"][0:1, :]),
                mark=f"sst_srow_r1_{ng}", sem="ssr", dma=True)

        for b in range(B_LOC):
            def x_load(j):
                q = b * NT + j
                if j < 2 and b == 1:
                    em.wait("ve_wev0_15_1")
                    em.wait_total("sw0")
                    em.wait_total("sw1")
                    em.wait("te_w0_15_1")
                if q >= 2:
                    em.wait(f"ve_xnbf{q - 2}")
                    em.wait(f"sc_sq2_{q - 2}")
                    em.wait(f"gp_acc{q - 2}")
                em.op(lambda q=q: sy.dma_start(
                    out=S[f"x_in{q % 2}"][:],
                    in_=AP(dx.tensor, q * 128 * D, [[D, 128], [1, D]])),
                    mark=f"sx_x{q}", sem=f"sx{q % 2}", dma=True)

            def p_stores(j):
                q = b * NT + j
                em.wait(f"gp_acc{q}")
                em.op(lambda q=q: sy.dma_start(
                    out=AP(dacc.tensor, q * 128 * D, [[D, 128], [1, D]]),
                    in_=S[f"accp{q % 2}"][:]),
                    mark=f"sst_acc{q}", sem=f"sp{q % 2}", dma=True)
                em.wait(f"ve_xnbf{q}")
                em.op(lambda q=q: sy.dma_start(
                    out=AP(dxnb.tensor, q * 128 * D, [[D, 128], [1, D]]),
                    in_=S[f"xnbf{q % 2}"][:]),
                    mark=f"sst_xnt{q}", sem=f"sn{q % 2}", dma=True)

            for j in range(NT):
                x_load(j)
                if j >= 1:
                    p_stores(j - 1)
            p_stores(NT - 1)

            for cv in range(2):
                for cg in range(32):
                    for ch in (1,):
                        prev = cg * 2 + ch - 4
                        if prev >= 0:
                            em.wait(f"te_cv{b}_{cv}_{prev // 2}")
                        elif cv == 1:
                            em.wait(f"te_cv{b}_0_{(64 + prev) // 2}")
                        else:
                            em.wait("wd_k1")
                            em.wait(f"te_proj{b}_15_5")
                        em.wait(f"gp_txe{cv}")
                        em.op(lambda cv=cv, ch=ch, cg=cg,
                              slot=(cg * 2 + ch) % 4: sy.dma_start(
                            out=S[f"toep{slot}"][:],
                            in_=AP(dtoep.tensor,
                                   ((cv * 2 + ch) * 32 + cg) * 128 * 4096,
                                   [[4096, 128], [1, 4096]])),
                            mark=f"wd_tp{b}_{cv}_{cg}_{ch}",
                            sem=f"stp{(cg * 2 + ch) % 4}", dma=True)

            def acc_load(j):
                q = b * NT + j
                em.wait_total("sp0")
                em.wait_total("sp1")
                if j < 2:
                    em.wait(f"te_r1_{b}_31")
                    em.wait(f"sst_xnt{qlast(b)}")
                else:
                    em.wait(f"ve_wev{b}_{j - 2}_1")
                em.op(lambda q=q, j=j: sy.dma_start(
                    out=S[f"accin{j % 2}"][:],
                    in_=AP(dacc.tensor, q * 128 * D, [[D, 128], [1, D]])),
                    mark=f"sx_accin{q}", sem=f"si{j % 2}", dma=True)

            def acc_store(j):
                q = b * NT + j
                em.wait(f"ve_wev{b}_{j}_1")
                em.op(lambda q=q, j=j: sy.dma_start(
                    out=AP(dacc.tensor, q * 128 * D, [[D, 128], [1, D]]),
                    in_=S[f"accout{j % 2}"][:]),
                    mark=f"sst_accw{q}", sem=f"sw{j % 2}", dma=True)

            for j in range(NT):
                acc_load(j)
                if j >= 1:
                    acc_store(j - 1)
            acc_store(NT - 1)

        if _DEBUG:
            em.wait("ve_y1_done")
            em.op(lambda: sy.dma_start(out=dcoef, in_=S["coef"][:]),
                  mark="sst_dbgcoef", sem="sdb", dma=True)
            em.op(lambda: sy.dma_start(out=dvrev, in_=S["vrev"][:]),
                  mark="sst_dbgvrev", sem="sdb", dma=True)
            em.op(lambda: sy.dma_start(out=dy2, in_=S["y2tm"][:]),
                  mark="sst_dbgy2", sem="sdb", dma=True)

        for gi in range(NGRP):
            if gi == 0:
                em.wait("ve_y1_done")
                em.wait("sc_sq2_31")
                em.wait("gp_acc31")
                em.wait("te_proj1_15_5")
                em.wait("sst_xnt30")
                em.wait("sst_xnt31")
            else:
                em.wait(f"te_gmm{gi - 1}_{NE - 1}_{NST - 1}")
            em.op(lambda gi=gi: sy.dma_start(
                out=S["xnbg"][:].rearrange("p (t c) -> p t c", t=8),
                in_=AP(dxnb.tensor, gi * 8 * 128 * D, [[D, 128], [128 * D, 8], [1, D]])),
                mark=f"wd_xnbg{gi}", sem="sxg", dma=True)
            if gi == 0:
                em.wait("te_cv1_1_31")
            else:
                em.wait(f"sout_{gi - 1}")
            em.wait(f"sst_accw{gi * 8 + 6}")
            em.wait(f"sst_accw{gi * 8 + 7}")
            em.op(lambda gi=gi: sy.dma_start(
                out=S["ygrp"][:].rearrange("p (t c) -> p t c", t=8),
                in_=AP(dacc.tensor, gi * 8 * 128 * D, [[D, 128], [128 * D, 8], [1, D]])),
                mark=f"wd_yg{gi}", sem="syg", dma=True)
            for e in range(NE):
                ge = gi * NE + e
                for k in range(0, 8, 2):
                    if ge == 0 and k < 4:
                        em.wait("ve_y1_done")
                        em.wait("te_y2tr1_15")
                        em.wait_total("scv2")
                        if _DEBUG:
                            em.wait("sst_dbgy2")
                    elif k < 4:
                        pgi, pe = divmod(ge - 1, NE)
                        phx = 4 * k + 19
                        em.wait(f"te_bphi{pgi}_{pe}_{phx}" if phx < 31
                                else f"te_b{pgi}_{pe}_5")
                    else:
                        em.wait(f"te_bphi{gi}_{e}_{4 * (k - 4) + 3}")
                    em.op(lambda e=e, k=k: sy.dma_start(
                        out=S[f"w2c{k % 4}"][:].rearrange("p (j c) -> p j c", j=4),
                        in_=AP(dw2b.tensor, e * FF * D + k * 4 * 128 * D,
                               [[D, 128], [128 * D, 4], [1, D]])),
                        mark=f"wd_w2_{gi}_{e}_{k}", sem=f"sw2{k % 4}", dma=True)
            em.wait(f"ve_ysum{gi}_{NE - 1}_7_1")
            em.op(lambda gi=gi: sy.dma_start(
                out=AP(dout.tensor, gi * 8 * 128 * D, [[D, 128], [128 * D, 8], [1, D]]),
                in_=S["ygrp"][:].rearrange("p (t c) -> p t c", t=8)),
                mark=f"sout_{gi}", sem="sout", dma=True)
        em.wait_total("sout")
        for sn_ in ("sp0", "sp1", "sn0", "sn1", "sw0", "sw1", "ssr", "sdb",
                    "sxg", "syg"):
            em.wait_total(sn_)

    def body_tensor(t):
        em.cur_engine = t

        def mm(out, lhsT, rhs, start, stop, mark=None, sem="ste"):
            em.op(lambda: t.matmul(out, lhsT, rhs, start=start, stop=stop,
                                   skip_group_check=True),
                  mark=mark, sem=sem)

        em.wait("gp_const")
        em.wait("ve_ones")
        for b in range(B_LOC):
            if b == 0:
                for cc in range(DCH):
                    em.wait(f"ve_wp0_{cc}")
                    em.wait("wd_k1")
                    for ng in range(6):
                        mm(PB[ng][0:1, :], S["bcol"][:, cc:cc + 1],
                           S[f"tmp16_{cc % 2}"][:, ng * 512:(ng + 1) * 512],
                           start=(cc == 0), stop=False,
                           mark=(f"te_biasc{cc}" if ng == 5 else None), sem="ste")
                for ng in range(6):
                    mm(PB[ng][0:1, :], S["ones"][0:1, 0:1],
                       S["bi"][0:1, ng * 512:(ng + 1) * 512],
                       start=False, stop=True, mark=f"te_bias{ng}", sem="ste")
                for ng in range(6):
                    em.wait(f"ve_srow_bias{ng}")
                    for cc in range(DCH):
                        mm(PB[ng][0:1, :], S["ones"][:, 0:1],
                           S["wp"][:, cc * PROJ + ng * 512: cc * PROJ + (ng + 1) * 512],
                           start=(cc == 0), stop=(cc == DCH - 1),
                           mark=(f"te_sp{ng}" if cc == DCH - 1 else None), sem="ste")

            for j in range(NT):
                q = b * NT + j
                em.wait(f"ve_xnbf{q}")
                if q >= 1:
                    em.wait(f"sc_srt{q - 1}")
                if b == 1 and j == 0:
                    em.wait("ve_y2tev0_15")
                for cc in range(DCH):
                    em.op(lambda q=q, cc=cc: t.transpose(
                        pb7[:, cc * 128:(cc + 1) * 128],
                        S[f"xnbf{q % 2}"][:, cc * 128:(cc + 1) * 128],
                        S["ident"][:]),
                        mark=(f"te_tr{q}" if cc == DCH - 1 else None), sem="ste")
                em.wait(f"sc_xnt{q}")
                em.wait(f"ve_stat{q}")
                em.op(lambda: t.transpose(pb7[0:2, 0:128], S["statp"][:], S["ident"][:]),
                      mark=f"te_st{q}", sem="ste")
                em.wait(f"sc_srt{q}")
                if b == 0 and j == 0:
                    for ng in range(6):
                        em.wait(f"ve_srow_sp{ng}")
                    em.wait("sst_srow_r1_5")
                for ng in range(6):
                    bank = PB[ng % 2]
                    if q * 6 + ng >= 2:
                        pq, png = divmod(q * 6 + ng - 2, 6)
                        em.wait(f"ve_pevac{pq}_{png}")
                    for cc in range(DCH):
                        mm(bank, S[f"xnt{q % 2}"][:, cc * 128:(cc + 1) * 128],
                           S["wp"][:, cc * PROJ + ng * 512: cc * PROJ + (ng + 1) * 512],
                           start=(cc == 0), stop=False)
                    mm(bank, S["strows"][0:2, :],
                       S["srow"][:, ng * 512:(ng + 1) * 512],
                       start=False, stop=True, mark=f"te_proj{b}_{j}_{ng}", sem="ste")
                if q >= 1:
                    em.wait(f"ve_coef{q - 1}")
                for cc in range(DCH):
                    mm(PB[6][:, 0:8], S[f"xnt{q % 2}"][:, cc * 128:(cc + 1) * 128],
                       S["rw"][:, cc * 8:(cc + 1) * 8],
                       start=(cc == 0), stop=(cc == DCH - 1),
                       mark=(f"te_rt{q}" if cc == DCH - 1 else None), sem="ste")
                for h in range(2):
                    em.wait(f"ve_pevac{q}_{h}")
                    if q >= 1:
                        em.wait(f"sc_vrev{q - 1}_{h}")
                    elif b == 0:
                        pass
                    mm(PB[2 + h], S["rev"][:],
                       S[f"vtmp{q % 2}"][:, h * 512:(h + 1) * 512],
                       start=True, stop=True, mark=f"te_vrev{q}_{h}", sem="ste")

            for cv in range(2):
                src = S["vrev"] if cv == 0 else S["y1rev"]
                for cg in range(32):
                    bank = PB[cg % 2]
                    em.wait(f"wd_tp{b}_{cv}_{cg}_0")
                    em.wait(f"wd_tp{b}_{cv}_{cg}_1")
                    if cg == 0:
                        if cv == 0:
                            em.wait(f"ve_pevac{qlast(b)}_4")
                            em.wait(f"ve_pevac{qlast(b)}_5")
                            em.wait(f"sc_vrev{qlast(b)}_0")
                            em.wait(f"sc_vrev{qlast(b)}_1")
                        else:
                            em.wait(f"ve_r1ev_{b}_31")
                            em.wait(f"ve_g{b}_0_31")
                    if cg >= 2:
                        em.wait(f"ve_g{b}_{cv}_{cg - 2}")
                    for ch in range(32):
                        d = cg * 32 + ch
                        slot0 = (cg * 2) % 4
                        slot1 = (cg * 2 + 1) % 4
                        mm(bank[:, ch * 16: ch * 16 + 16],
                           S[f"toep{slot1}"][:, ch * 128:(ch + 1) * 128],
                           rearr(src)[:, d, 0:16],
                           start=(ch == 0), stop=False)
                        mm(bank[:, ch * 16 + 1: ch * 16 + 16],
                           S[f"toep{slot0}"][:, ch * 128:(ch + 1) * 128],
                           rearr(src)[:, d, 0:15],
                           start=False, stop=(ch == 31),
                           mark=(f"te_cv{b}_{cv}_{cg}" if ch == 31 else None), sem="ste")
                if cv == 0:
                    em.wait(f"ve_g{b}_0_31")
                    for h in range(32):
                        if h >= 2:
                            em.wait(f"ve_r1ev_{b}_{h - 2}")
                        else:
                            em.wait(f"sc_vrev{qlast(b)}_{h}")
                        mm(PB[2 + h % 2], S["rev"][:],
                           S["y1tm"][:, h * 512:(h + 1) * 512],
                           start=True, stop=True, mark=f"te_r1_{b}_{h}", sem="ste")

            em.wait(f"wd_wo{b}")
            em.wait(f"ve_g{b}_1_31")
            em.wait(f"sc_srt{qlast(b)}")
            for j in range(NT):
                q = b * NT + j
                if j >= 1:
                    em.wait(f"ve_y2tev{b}_{j - 1}")
                for cc in range(DCH):
                    em.op(lambda j=j, cc=cc: t.transpose(
                        pb7[:, cc * 128:(cc + 1) * 128],
                        S["y2tm"][:, j * 1024 + cc * 128: j * 1024 + (cc + 1) * 128],
                        S["ident"][:]),
                        mark=(f"te_y2tr{b}_{j}" if cc == DCH - 1 else None), sem="ste")
                em.wait(f"ve_y2tev{b}_{j}")
                for cg in range(2):
                    bank = PB[4 + cg]
                    if j >= 1:
                        em.wait(f"ve_wev{b}_{j - 1}_{cg}")
                    elif b == 1:
                        em.wait(f"ve_wev0_15_{cg}")
                    for cc in range(DCH):
                        mm(bank, S[f"y2t{j % 2}"][:, cc * 128:(cc + 1) * 128],
                           S["wo"][:, cc * D + cg * 512: cc * D + (cg + 1) * 512],
                           start=(cc == 0), stop=False)
                    mm(bank, S["ones"][0:1, 0:128],
                       S["bo"][0:1, cg * 512:(cg + 1) * 512],
                       start=False, stop=True, mark=f"te_w{b}_{j}_{cg}", sem="ste")

        em.wait("sb1_all")
        em.wait("ve_y2tev1_15")
        for phi in range(32):
            em.op(lambda phi=phi: t.transpose(
                pb7[:, phi * 8:(phi + 1) * 8],
                S["b1r"][0:8, phi * 128:(phi + 1) * 128], S["ident"][0:8, 0:8]),
                mark=("te_b1t" if phi == 31 else None), sem="ste")

        def mmf(out, lhsT, rhs, start, stop, mark=None):
            em.op(lambda: t.matmul(out, lhsT, rhs, start=start, stop=stop,
                                   skip_group_check=True), mark=mark, sem="ste")

        for gi in range(NGRP):
            em.wait(f"ve_mask{gi}")
            em.wait("gp_p2c")
            if gi == 0:
                em.wait("ve_b1c")
                em.wait("ve_cbf")
                em.wait("ve_coef31")
            else:
                em.wait(f"ve_ysb{gi - 1}_{NE - 1}_4")
            mmf(PB[6][0:1, 0:64], S["onesf"][:, 0:1], S["maskm"][:],
                start=True, stop=True, mark=f"te_cnt{gi}")
            em.wait(f"ve_cnt{gi}")
            em.wait(f"ve_base{gi}")
            mmf(PB[6][:, 0:64], S["ltri"][:], S["maskm"][:], start=True, stop=False)
            mmf(PB[6][:, 0:64], S["onesf"][0:1, 0:128], S["row3"][0:1, :],
                start=False, stop=True, mark=f"te_pos1{gi}")

            for e in range(NE):
                ge = gi * NE + e
                pgi, pe = divmod(ge - 1, NE)
                em.wait(f"ve_gt{gi}_{e}")
                if e == 0:
                    em.wait(f"wd_xnbg{gi}")
                if ge == 0:
                    em.wait("ve_g1_1_31")
                    em.wait("ve_r1ev_1_31")
                for sb in range(NST):
                    bk0, bk1 = GBANK(sb)
                    if sb == 2:
                        em.wait(f"ve_xgev{gi}_{e}_0")
                    elif ge > 0:
                        if sb == 0:
                            em.wait(f"ve_h{pgi}_{pe}_30")
                            em.wait(f"ve_ysum{pgi}_{pe}_6_0")
                        else:
                            em.wait(f"ve_ysum{pgi}_{pe}_6_1")
                            em.wait(f"ve_ysum{pgi}_{pe}_7_0")
                    for cg in range(2):
                        bk = bk0 if cg == 0 else bk1
                        for tt in range(8):
                            mmf(bk[:],
                                S[f"gt{e % 2}"][:, tt * 384 + sb * 128:
                                                tt * 384 + sb * 128 + 128],
                                S["xnbg"][:, tt * 1024 + cg * 512:
                                          tt * 1024 + (cg + 1) * 512],
                                start=(tt == 0), stop=(tt == 7))
                    if sb == 0 and ge > 0:
                        em.wait(f"ve_ysb{pgi}_{pe}_4")
                    for tt in range(8):
                        col = (gi * 8 + tt) * 8 + e
                        mmf(PB[6][:, 64 + sb:65 + sb],
                            S[f"gt{e % 2}"][:, tt * 384 + sb * 128:
                                            tt * 384 + sb * 128 + 128],
                            S["coefbf"][:, col:col + 1],
                            start=(tt == 0), stop=(tt == 7),
                            mark=(f"te_gmm{gi}_{e}_{sb}" if tt == 7 else None))
                for sb in range(NST):
                    em.wait(f"ve_xgev{gi}_{e}_{sb}")
                    if sb == 0:
                        if ge > 0:
                            em.wait(f"ve_ysb{pgi}_{pe}_5")
                    else:
                        em.wait(f"ve_xgtev{gi}_{e}_{sb - 1}")
                    for cc in range(DCH):
                        em.op(lambda sb=sb, cc=cc: t.transpose(
                            pb7[:, cc * 128:(cc + 1) * 128],
                            S[f"xgsb{sb % 2}"][:, cc * 128:(cc + 1) * 128],
                            S["ident"][:]),
                            mark=(f"te_xgtr{gi}_{e}_{sb}" if cc == DCH - 1 else None),
                            sem="ste")
                for sb in range(NST):
                    if sb == 0:
                        em.wait(f"ve_xgtev{gi}_{e}_2")
                    else:
                        em.wait(f"ve_gsev{gi}_{e}_{sb - 1}")
                    for tt in range(8):
                        em.op(lambda sb=sb, tt=tt, e=e: t.transpose(
                            pb7[:, tt * 128:(tt + 1) * 128],
                            S[f"gt{e % 2}"][:, tt * 384 + sb * 128:
                                            tt * 384 + sb * 128 + 128],
                            S["ident"][:]),
                            mark=(f"te_gstr{gi}_{e}_{sb}" if tt == 7 else None),
                            sem="ste")
                for phi in range(32):
                    em.wait(f"wd_w1_{gi}_{e}_{phi // 2}")
                    bank = PB[phi % 2]
                    if phi >= 2:
                        em.wait(f"ve_h{gi}_{e}_{phi - 2}")
                    else:
                        em.wait(f"ve_xgev{gi}_{e}_2")
                        if phi == 0:
                            em.wait(f"ve_xgtev{gi}_{e}_{NST - 1}")
                    for cc in range(DCH):
                        em.op(lambda phi=phi, cc=cc, e=e, bank=bank: t.matmul(
                            bank[:, 0:CAP],
                            S[f"w1c{(phi // 2) % 4}"][
                                :, cc * 256 + (phi % 2) * 128:
                                cc * 256 + (phi % 2) * 128 + 128],
                            S[f"xgt{e % 2}"][:, cc * CAP:(cc + 1) * CAP],
                            start=(cc == 0), stop=(cc == DCH - 1),
                            skip_group_check=True),
                            mark=(f"te_a{gi}_{e}_{phi}" if cc == DCH - 1 else None),
                            sem="ste")
                em.wait(f"wd_b2_{gi}_{e}")
                for ci, ch in enumerate(BCH):
                    if ci in (0, 1):
                        em.wait(f"ve_xgev{gi}_{e}_1")
                    elif ci == 2:
                        em.wait("ve_wev1_15_0" if ge == 0
                                else f"ve_ysum{pgi}_{pe}_7_1")
                    elif ci == 3:
                        em.wait("ve_wev1_15_1" if ge == 0
                                else f"ve_ysb{pgi}_{pe}_3")
                    elif ci == 4:
                        em.wait(f"ve_xgev{gi}_{e}_2")
                    else:
                        em.wait(f"ve_gsev{gi}_{e}_2")
                    em.op(lambda ci=ci, ch=ch, e=e: t.matmul(
                        (pb7f if ch == 7 else PB[ch])[:], S["ones"][0:1, 0:128],
                        S[f"b2r{e % 2}"][0:1, (ci % 2) * 512:(ci % 2) * 512 + 512],
                        start=True, stop=False, skip_group_check=True), sem="ste")
                for phi in range(32):
                    em.wait(f"ve_h{gi}_{e}_{phi}")
                    if phi % 4 == 0:
                        em.wait(f"wd_w2_{gi}_{e}_{phi // 4}")
                    for ci, ch in enumerate(BCH):
                        st, cg = ci // 2, ci % 2
                        if phi == 31:
                            mk = f"te_b{gi}_{e}_{ci}"
                        else:
                            mk = f"te_bphi{gi}_{e}_{phi}" if ci == 5 else None
                        em.op(lambda phi=phi, st=st, cg=cg, ch=ch: t.matmul(
                            (pb7f if ch == 7 else PB[ch])[:],
                            S["hT"][:, phi * CAP + st * 128:
                                    phi * CAP + st * 128 + 128],
                            S[f"w2c{(phi // 4) % 4}"][:, (phi % 4) * 1024 + cg * 512:
                                                      (phi % 4) * 1024 + (cg + 1) * 512],
                            start=False, stop=(phi == 31), skip_group_check=True),
                            mark=mk, sem="ste")
                em.wait(f"ve_gsev{gi}_{e}_2")
                for tt in range(8):
                    for cg in range(2):
                        bank = PB[1 + 2 * (tt % 2) + cg]
                        em.wait(f"ve_ysb{gi}_{e}_{4 + cg}")
                        if tt >= 2:
                            em.wait(f"ve_ysum{gi}_{e}_{tt - 2}_{cg}")
                        elif tt == 0 and cg == 0:
                            em.wait(f"ve_h{gi}_{e}_31")
                        elif tt == 0 and cg == 1:
                            em.wait(f"ve_ysb{gi}_{e}_0")
                        elif tt == 1 and cg == 0:
                            em.wait(f"ve_ysb{gi}_{e}_1")
                        else:
                            em.wait(f"ve_ysb{gi}_{e}_2")
                        for sb in range(NST):
                            em.op(lambda e=e, tt=tt, cg=cg, sb=sb, bank=bank:
                                  t.matmul(
                                      bank[:],
                                      S[f"gs{e % 2}"][:, sb * 1024 + tt * 128:
                                                      sb * 1024 + tt * 128 + 128],
                                      S[f"ysb{e % 2}"][:, sb * 1024 + cg * 512:
                                                       sb * 1024 + (cg + 1) * 512],
                                      start=(sb == 0), stop=(sb == NST - 1),
                                      skip_group_check=True),
                                  mark=(f"te_sc{gi}_{e}_{tt}_{cg}"
                                        if sb == NST - 1 else None),
                                  sem="ste")

    def body_vector(v):
        em.cur_engine = v

        def ts(out, in0, s1, s2, o1, o2=None):
            if o2 is None:
                return v.tensor_scalar(out, in0, s1, None, o1)
            return v.tensor_scalar(out, in0, s1, s2, o1, o2)

        em.op(lambda: v.memset(S["xsq"][:], 0.0), mark="ve_zrow", sem="sve")
        em.op(lambda: v.memset(S["ones"][:], 1.0), mark="ve_ones", sem="sve")

        for b in range(B_LOC):
            for cc in range(DCH):
                em.wait(f"wd_winc{b}_{cc}")
                if b == 0 and cc == 0:
                    em.wait("wd_k1")
                if b == 1 and cc == 0:
                    em.wait("te_cv0_1_31")
                em.op(lambda b=b, cc=cc: ts(
                    S["wp"][:, cc * PROJ:(cc + 1) * PROJ],
                    S[f"tmp16_{cc % 2}"][:], S["gcol"][:, cc:cc + 1], None, ALU.mult),
                    mark=f"ve_wp{b}_{cc}", sem="sve")
            if b == 0:
                for ng in range(6):
                    em.wait(f"te_bias{ng}")
                    em.op(lambda ng=ng: v.tensor_copy(S["srtmp"][0:1, :], PB[ng][0:1, :]),
                          mark=f"ve_srow_bias{ng}", sem="sve")
                    if ng < 5:
                        em.wait(f"sst_srow_r1_{ng}")
                for ng in range(6):
                    em.wait(f"te_sp{ng}")
                    em.op(lambda ng=ng: v.tensor_copy(
                        S["srow"][0:1, ng * 512:(ng + 1) * 512], PB[ng][0:1, :]),
                        mark=f"ve_srow_sp{ng}", sem="sve")

            for j in range(NT):
                q = b * NT + j
                X = S[f"x_in{q % 2}"]
                if b == 1 and j == 0:
                    em.wait("sst_accw14")
                    em.wait("sst_accw15")
                    em.wait("te_w0_15_1")
                em.wait(f"sx_x{q}")
                em.op(lambda X=X: v.reduce_sum(S["scr"][:, 0:1], X[:], axis=AX.X))
                em.op(lambda: v.drain())
                em.op(lambda: ts(S["scr"][:, 1:2], S["scr"][:, 0:1], 1.0 / D, None,
                                 ALU.mult))
                em.op(lambda: v.drain())
                em.op(lambda: ts(S["scr"][:, 2:3], S["scr"][:, 1:2], S["scr"][:, 1:2],
                                 None, ALU.mult))
                em.op(lambda: v.drain())
                em.wait(f"sc_sq1_{q}")
                em.op(lambda: v.scalar_tensor_tensor(
                    S["scr"][:, 3:4], S["scr"][:, 4:5], 1.0 / D, S["scr"][:, 2:3],
                    ALU.mult, ALU.subtract))
                em.op(lambda: v.drain())
                em.op(lambda: ts(S["scr"][:, 5:6], S["scr"][:, 3:4], EPS, None, ALU.add),
                      mark=f"ve_var1_{q}", sem="sve")
                em.wait(f"sc_ln1_{q}")
                em.op(lambda: ts(S["scr"][:, 7:8], S["scr"][:, 6:7], -0.5, None, ALU.mult),
                      mark=f"ve_lnh1_{q}", sem="sve")
                em.wait(f"sc_r1_{q}")
                if q >= 2:
                    em.wait(f"sc_sq2_{q - 2}")
                    em.wait(f"gp_acc{q - 2}")
                em.op(lambda X=X, q=q: ts(
                    S[f"xn32{q % 2}"][:], X[:], S["scr"][:, 1:2], S["scr"][:, 8:9],
                    ALU.subtract, ALU.mult), mark=f"ve_xn{q}", sem="sve")
                if q >= 2:
                    em.wait(f"te_tr{q - 2}")
                    em.wait(f"sst_xnt{q - 2}")
                em.op(lambda X=X, q=q: ts(
                    S[f"xnbf{q % 2}"][:], X[:], S["scr"][:, 1:2], S["scr"][:, 8:9],
                    ALU.subtract, ALU.mult), mark=f"ve_xnbf{q}", sem="sve")
                em.op(lambda: v.drain())
                if q >= 2:
                    em.wait(f"sst_acc{q - 2}")
                em.op(lambda X=X, q=q: v.tensor_add(
                    S[f"accp{q % 2}"][:], X[:], S[f"xn32{q % 2}"][:]),
                    mark=f"gp_acc{q}", sem="sve")
                em.op(lambda q=q: v.reduce_sum(S["scr"][:, 9:10],
                                               S[f"xn32{q % 2}"][:], axis=AX.X))
                em.op(lambda: v.drain())
                em.op(lambda: ts(S["scr"][:, 10:11], S["scr"][:, 9:10], 1.0 / D, None,
                                 ALU.mult))
                em.op(lambda: v.drain())
                em.op(lambda: ts(S["scr"][:, 11:12], S["scr"][:, 10:11], S["scr"][:, 10:11],
                                 None, ALU.mult))
                em.op(lambda: v.drain())
                em.wait(f"sc_sq2_{q}")
                em.op(lambda: v.scalar_tensor_tensor(
                    S["scr"][:, 12:13], S["scr"][:, 13:14], 1.0 / D, S["scr"][:, 11:12],
                    ALU.mult, ALU.subtract))
                em.op(lambda: v.drain())
                em.op(lambda: ts(S["scr"][:, 14:15], S["scr"][:, 12:13], EPS, None, ALU.add),
                      mark=f"ve_var2_{q}", sem="sve")
                em.wait(f"sc_ln2_{q}")
                em.op(lambda: ts(S["scr"][:, 16:17], S["scr"][:, 15:16], -0.5, None,
                                 ALU.mult), mark=f"ve_lnh2_{q}", sem="sve")
                em.wait(f"sc_r2_{q}")
                em.op(lambda: v.reciprocal(S["scr"][:, 18:19], S["scr"][:, 17:18]))
                em.op(lambda: v.drain())
                em.op(lambda: ts(S["scr"][:, 19:20], S["scr"][:, 10:11], -1.0, None,
                                 ALU.mult))
                em.op(lambda: v.drain())
                if q >= 1:
                    em.wait(f"te_st{q - 1}")
                em.op(lambda: v.tensor_copy(S["statp"][:, 0:1], S["scr"][:, 19:20]))
                em.op(lambda: v.tensor_copy(S["statp"][:, 1:2], S["scr"][:, 18:19]))
                em.op(lambda: v.drain(), mark=f"ve_stat{q}", sem="sve")
                for ng in range(6):
                    em.wait(f"te_proj{b}_{j}_{ng}")
                    if ng < 2:
                        if q >= 2:
                            em.wait(f"te_vrev{q - 2}_{ng}")
                        dst = S[f"vtmp{q % 2}"][:, ng * 512:(ng + 1) * 512]
                    elif ng < 4:
                        dst = S["x1tm"][:, j * 1024 + (ng - 2) * 512:
                                        j * 1024 + (ng - 1) * 512]
                    else:
                        dst = S["x2tm"][:, j * 1024 + (ng - 4) * 512:
                                        j * 1024 + (ng - 3) * 512]
                    em.op(lambda dst=dst, ng=ng: ts(
                        dst, PB[ng % 2][:], S["scr"][:, 17:18], None, ALU.mult),
                        mark=f"ve_pevac{q}_{ng}", sem="sve")
                em.wait(f"te_rt{q}")
                em.op(lambda: v.max(out=S["scr2"][:, 0:8], in_=PB[6][:, 0:8]))
                em.op(lambda: v.drain())
                em.op(lambda: ts(S["scr2"][:, 8:16], PB[6][:, 0:8], S["scr2"][:, 0:1],
                                 None, ALU.subtract), mark=f"ve_lsub{q}", sem="sve")
                em.op(lambda: ts(S["scr2"][:, 16:24], PB[6][:, 0:8], S["scr2"][:, 1:2],
                                 None, ALU.is_ge))
                em.op(lambda: v.drain())
                em.wait(f"sc_exp{q}")
                em.op(lambda: v.reduce_sum(S["scr2"][:, 28:29], S["rt8"][:], axis=AX.X))
                em.op(lambda: v.drain())
                em.op(lambda: v.tensor_mul(S["rt8"][:], S["rt8"][:], S["scr2"][:, 16:24]))
                em.op(lambda: v.drain())
                em.op(lambda: v.reduce_sum(S["scr2"][:, 24:25], S["rt8"][:], axis=AX.X))
                em.op(lambda: v.drain())
                em.op(lambda: v.scalar_tensor_tensor(
                    S["scr2"][:, 25:26], S["scr2"][:, 28:29], 1e-9, S["scr2"][:, 24:25],
                    ALU.mult, ALU.add))
                em.op(lambda: v.drain())
                em.op(lambda: v.reciprocal(S["scr2"][:, 27:28], S["scr2"][:, 25:26]))
                em.op(lambda: v.drain())
                em.op(lambda q=q: ts(S["coef"][:, q * 8:(q + 1) * 8], S["rt8"][:],
                                     S["scr2"][:, 27:28], None, ALU.mult),
                      mark=f"ve_coef{q}", sem="sve")

            def gate_evacs(cv):
                gate = S["x1tm"] if cv == 0 else S["x2tm"]
                ydst = S["y1tm"] if cv == 0 else S["y2tm"]
                for cg in range(32):
                    em.wait(f"te_cv{b}_{cv}_{cg}")
                    if cv == 0 and cg == 0:
                        em.wait(f"sst_acc{qlast(b) - 1}")
                        em.wait(f"sst_acc{qlast(b)}")
                        em.wait(f"sst_xnt{qlast(b) - 1}")
                        em.wait(f"sst_xnt{qlast(b)}")
                    em.op(lambda cg=cg, gate=gate, ydst=ydst, bank=PB[cg % 2]:
                          v.tensor_mul(
                              rearr(ydst)[:, cg * 32:(cg + 1) * 32, :],
                              bank[:].rearrange("p (c j) -> p c j", c=32),
                              rearr(gate)[:, cg * 32:(cg + 1) * 32, :]),
                          mark=f"ve_g{b}_{cv}_{cg}", sem="sve")
            gate_evacs(0)
            for h in range(32):
                em.wait(f"te_r1_{b}_{h}")
                em.op(lambda h=h: v.tensor_copy(
                    S["y1rev"][:, h * 512:(h + 1) * 512], PB[2 + h % 2][:]),
                    mark=f"ve_r1ev_{b}_{h}", sem="sve")
            gate_evacs(1)
            for j in range(NT):
                q = b * NT + j
                em.wait(f"te_y2tr{b}_{j}")
                if j >= 2:
                    em.wait(f"te_w{b}_{j - 2}_1")
                em.op(lambda j=j: v.tensor_copy(S[f"y2t{j % 2}"][:], pb7[:]),
                      mark=f"ve_y2tev{b}_{j}", sem="sve")
                for cg in range(2):
                    em.wait(f"te_w{b}_{j}_{cg}")
                    em.wait(f"sx_accin{q}")
                    if j == 0:
                        em.wait(f"sc_exp{qlast(b)}")
                        if b == 1:
                            em.wait("sst_accw14")
                            em.wait("sst_accw15")
                    if j >= 2:
                        em.wait(f"sst_accw{q - 2}")
                    em.op(lambda j=j, cg=cg: v.scalar_tensor_tensor(
                        S[f"accout{j % 2}"][:, cg * 512:(cg + 1) * 512],
                        PB[4 + cg][:], 1.0,
                        S[f"accin{j % 2}"][:, cg * 512:(cg + 1) * 512],
                        ALU.mult, ALU.add),
                        mark=f"ve_wev{b}_{j}_{cg}", sem="sve")

        em.op(lambda: v.drain(), mark="ve_y1_done", sem="sve")

        em.wait("sst_acc30")
        em.wait("sst_acc31")
        em.wait("sst_accw30")
        em.wait("sst_accw31")
        em.wait("te_b1t")
        em.op(lambda: v.tensor_copy(S["b1call"][:], pb7[:, 0:256]),
              mark="ve_b1c", sem="sve")
        em.op(lambda: v.tensor_copy(S["coefbf"][:], S["coef"][:]),
              mark="ve_cbf", sem="sve")

        for gi in range(NGRP):
            CS = S["coef"][:, gi * 64:(gi + 1) * 64]

            def gt_build(e, gi=gi):
                ge = gi * NE + e
                if ge >= 2:
                    p2g, p2e = divmod(ge - 2, NE)
                    em.wait(f"te_gstr{p2g}_{p2e}_2")
                for tt in range(8):
                    em.op(lambda e=e, tt=tt: v.tensor_scalar(
                        S[f"gt{e % 2}"][:, tt * 384:(tt + 1) * 384],
                        S["iot"][:], S["posd1"][:, tt * 8 + e: tt * 8 + e + 1],
                        None, ALU.is_equal),
                        mark=(f"ve_gt{gi}_{e}" if tt == 7 else None), sem="sve")

            if gi > 0:
                em.wait(f"te_pos1{gi - 1}")
            em.op(lambda CS=CS: v.tensor_scalar(S["maskm"][:], CS, 0.0, None,
                                                ALU.not_equal),
                  mark=f"ve_mask{gi}", sem="sve")
            em.wait(f"te_cnt{gi}")
            em.op(lambda: v.tensor_copy(S["row0"][0:1, :], PB[6][0:1, 0:64]),
                  mark=f"ve_cnt{gi}", sem="sve")
            em.op(lambda: v.drain())
            em.op(lambda: v.memset(S["row1"][0:1, 0:8], 0.0))
            em.op(lambda: v.tensor_copy(S["row1"][0:1, 8:64], S["row0"][0:1, 0:56]))
            em.op(lambda: v.drain())
            em.op(lambda: v.tensor_copy(S["row2"][0:1, :], S["row1"][0:1, :]))
            em.op(lambda: v.drain())
            em.op(lambda: v.tensor_add(S["row2"][0:1, 8:64], S["row1"][0:1, 8:64],
                                       S["row1"][0:1, 0:56]))
            em.op(lambda: v.drain())
            em.op(lambda: v.tensor_copy(S["row3"][0:1, :], S["row2"][0:1, :]))
            em.op(lambda: v.drain())
            em.op(lambda: v.tensor_add(S["row3"][0:1, 16:64], S["row2"][0:1, 16:64],
                                       S["row2"][0:1, 0:48]))
            em.op(lambda: v.drain())
            em.op(lambda: v.tensor_add(S["row3"][0:1, 32:64], S["row3"][0:1, 32:64],
                                       S["row3"][0:1, 0:32]))
            em.op(lambda: v.drain(), mark=f"ve_base{gi}", sem="sve")
            em.wait(f"te_pos1{gi}")
            em.op(lambda: v.tensor_copy(S["posd0"][:], PB[6][:, 0:64]))
            em.op(lambda: v.drain())
            em.op(lambda: v.tensor_scalar(S["tt0"][:], S["maskm"][:], -1.0e9, 1.0e9,
                                          ALU.mult, ALU.add))
            em.op(lambda: v.tensor_scalar(S["tt1"][:], S["posd0"][:], float(CAP),
                                          1.0e9, ALU.is_ge, ALU.mult))
            em.op(lambda: v.drain())
            em.op(lambda: v.tensor_add(S["tt2"][:], S["tt0"][:], S["tt1"][:]))
            em.op(lambda: v.drain())
            em.op(lambda: v.tensor_add(S["posd1"][:], S["posd0"][:], S["tt2"][:]))
            em.op(lambda: v.drain(), mark=f"ve_pd{gi}", sem="sve")
            gt_build(0)

            for e in range(NE):
                ge = gi * NE + e
                pgi, pe = divmod(ge - 1, NE)
                for sb in range(NST):
                    em.wait(f"te_gmm{gi}_{e}_{sb}")
                    if ge == 0 and sb == 0:
                        em.wait("te_cv1_1_31")
                    if sb == 2:
                        em.wait(f"te_xgtr{gi}_{e}_0")
                    elif ge > 0:
                        em.wait(f"te_xgtr{pgi}_{pe}_{2 - sb}")
                    bk0, bk1 = GBANK(sb)
                    em.op(lambda sb=sb, bk0=bk0: v.tensor_copy(
                        S[f"xgsb{sb % 2}"][:, 0:512], bk0[:]))
                    em.op(lambda sb=sb, bk1=bk1: v.tensor_copy(
                        S[f"xgsb{sb % 2}"][:, 512:1024], bk1[:]))
                    em.op(lambda e=e, sb=sb: v.tensor_copy(
                        S[f"gcoefc{e % 2}"][:, sb:sb + 1], PB[6][:, 64 + sb:65 + sb]),
                        mark=f"ve_xgev{gi}_{e}_{sb}", sem="sve")
                for sb in range(NST):
                    em.wait(f"te_xgtr{gi}_{e}_{sb}")
                    if sb == 0:
                        if ge == 0:
                            em.wait("te_proj1_15_5")
                        if ge >= 2:
                            p2g, p2e = divmod(ge - 2, NE)
                            em.wait(f"te_a{p2g}_{p2e}_31")
                    em.op(lambda e=e, sb=sb: v.tensor_copy(
                        S[f"xgt{e % 2}"][:].rearrange("p (c s) -> p c s", c=DCH)
                        [:, :, sb * 128:(sb + 1) * 128],
                        pb7[:].rearrange("p (c s) -> p c s", c=DCH)),
                        mark=f"ve_xgtev{gi}_{e}_{sb}", sem="sve")
                if e + 1 < NE:
                    gt_build(e + 1)
                for sb in range(NST):
                    em.wait(f"te_gstr{gi}_{e}_{sb}")
                    if sb == 0 and ge >= 2:
                        p2g, p2e = divmod(ge - 2, NE)
                        em.wait(f"te_sc{p2g}_{p2e}_7_1")
                    em.op(lambda e=e, sb=sb: v.tensor_copy(
                        S[f"gs{e % 2}"][:, sb * 1024:(sb + 1) * 1024], pb7[:]),
                        mark=f"ve_gsev{gi}_{e}_{sb}", sem="sve")
                for phi in range(32):
                    em.wait(f"te_a{gi}_{e}_{phi}")
                    if ge == 0 and phi == 0:
                        em.wait("te_y2tr1_15")
                    if ge > 0:
                        em.wait(f"te_bphi{pgi}_{pe}_{phi}" if phi < 31
                                else f"te_b{pgi}_{pe}_5")
                    em.op(lambda gi=gi, e=e, phi=phi: v.tensor_scalar(
                        S["hT"][:, phi * CAP:(phi + 1) * CAP],
                        PB[phi % 2][:, 0:CAP],
                        S["b1call"][:, e * 32 + phi: e * 32 + phi + 1], 0.0,
                        ALU.add, ALU.max),
                        mark=f"ve_h{gi}_{e}_{phi}", sem="sve")
                first_ci = True
                for ci in (0, 2, 4, 1, 3, 5):
                    st, cg = ci // 2, ci % 2
                    em.wait(f"te_b{gi}_{e}_{ci}")
                    if first_ci and ge >= 2:
                        p2g, p2e = divmod(ge - 2, NE)
                        em.wait(f"te_sc{p2g}_{p2e}_7_1")
                    first_ci = False
                    em.op(lambda e=e, st=st, cg=cg, ci=ci: v.tensor_scalar(
                        S[f"ysb{e % 2}"][:, st * 1024 + cg * 512:
                                         st * 1024 + (cg + 1) * 512],
                        (pb7f if BCH[ci] == 7 else PB[BCH[ci]])[:],
                        S[f"gcoefc{e % 2}"][:, st:st + 1], None, ALU.mult),
                        mark=f"ve_ysb{gi}_{e}_{ci}", sem="sve")
                for tt in range(8):
                    for cg in range(2):
                        em.wait(f"te_sc{gi}_{e}_{tt}_{cg}")
                        if e == 0 and tt == 0 and cg == 0:
                            em.wait(f"wd_yg{gi}")
                        em.op(lambda tt=tt, cg=cg: v.tensor_add(
                            S["ygrp"][:, tt * 1024 + cg * 512:
                                      tt * 1024 + (cg + 1) * 512],
                            S["ygrp"][:, tt * 1024 + cg * 512:
                                      tt * 1024 + (cg + 1) * 512],
                            PB[1 + 2 * (tt % 2) + cg][:]),
                            mark=f"ve_ysum{gi}_{e}_{tt}_{cg}", sem="sve")

    def body_scalar(s):
        em.cur_engine = s
        for b in range(B_LOC):
            for j in range(NT):
                q = b * NT + j
                X = S[f"x_in{q % 2}"]
                em.wait(f"sx_x{q}")
                if q == 0:
                    em.wait("wd_k1")
                em.op(lambda X=X: s.activation(S["xsq"][:], X[:], AF.Square,
                                               accum_out=S["scr"][:, 4:5]),
                      mark=f"sc_sq1_{q}", sem="ssc")
                em.wait(f"ve_var1_{q}")
                em.op(lambda: s.activation(S["scr"][:, 6:7], S["scr"][:, 5:6], AF.Ln),
                      mark=f"sc_ln1_{q}", sem="ssc")
                em.wait(f"ve_lnh1_{q}")
                em.op(lambda: s.activation(S["scr"][:, 8:9], S["scr"][:, 7:8], AF.Exp),
                      mark=f"sc_r1_{q}", sem="ssc")
                if j >= 1:
                    for h in range(2):
                        em.wait(f"te_vrev{q - 1}_{h}")
                        pj = (q - 1) % NT
                        em.op(lambda pj=pj, h=h: s.activation(
                            S["vrev"][:, pj * 1024 + h * 512: pj * 1024 + (h + 1) * 512],
                            PB[2 + h][:], AF.Copy),
                            mark=f"sc_vrev{q - 1}_{h}", sem="ssc")
                em.wait(f"ve_xn{q}")
                em.op(lambda q=q: s.activation(S["xsq"][:], S[f"xn32{q % 2}"][:],
                                               AF.Square, accum_out=S["scr"][:, 13:14]),
                      mark=f"sc_sq2_{q}", sem="ssc")
                em.wait(f"ve_var2_{q}")
                em.op(lambda: s.activation(S["scr"][:, 15:16], S["scr"][:, 14:15], AF.Ln),
                      mark=f"sc_ln2_{q}", sem="ssc")
                em.wait(f"ve_lnh2_{q}")
                em.op(lambda: s.activation(S["scr"][:, 17:18], S["scr"][:, 16:17], AF.Exp),
                      mark=f"sc_r2_{q}", sem="ssc")
                em.wait(f"te_tr{q}")
                if q >= 2:
                    pq = q - 2
                    em.wait(f"te_proj{pq // NT}_{pq % NT}_5")
                    em.wait(f"te_rt{pq}")
                em.op(lambda q=q: s.activation(S[f"xnt{q % 2}"][:], pb7[:], AF.Copy),
                      mark=f"sc_xnt{q}", sem="ssc")
                em.wait(f"te_st{q}")
                if q >= 1:
                    pq = q - 1
                    em.wait(f"te_proj{pq // NT}_{pq % NT}_5")
                em.op(lambda: s.activation(S["strows"][0:2, :], pb7[0:2, 0:128],
                                           AF.Copy),
                      mark=f"sc_srt{q}", sem="ssc")
                em.wait(f"ve_lsub{q}")
                em.op(lambda: s.activation(S["rt8"][:], S["scr2"][:, 8:16], AF.Exp),
                      mark=f"sc_exp{q}", sem="ssc")
                if j == NT - 1:
                    for h in range(2):
                        em.wait(f"te_vrev{q}_{h}")
                        em.op(lambda j=j, h=h: s.activation(
                            S["vrev"][:, j * 1024 + h * 512: j * 1024 + (h + 1) * 512],
                            PB[2 + h][:], AF.Copy),
                            mark=f"sc_vrev{q}_{h}", sem="ssc")
            for cv in range(2):
                for cg in range(32):
                    for ch in (0,):
                        prev = cg * 2 + ch - 4
                        if prev >= 0:
                            em.wait(f"te_cv{b}_{cv}_{prev // 2}")
                        elif cv == 1:
                            em.wait(f"te_cv{b}_0_{(64 + prev) // 2}")
                        else:
                            em.wait("wd_k1")
                            em.wait(f"te_proj{b}_15_5")
                        em.wait(f"gp_txe{cv}")
                        em.op(lambda cv=cv, ch=ch, cg=cg,
                              slot=(cg * 2 + ch) % 4: s.dma_start(
                            out=S[f"toep{slot}"][:],
                            in_=AP(dtoep.tensor,
                                   ((cv * 2 + ch) * 32 + cg) * 128 * 4096,
                                   [[4096, 128], [1, 4096]])),
                            mark=f"wd_tp{b}_{cv}_{cg}_{ch}",
                            sem=f"stp{(cg * 2 + ch) % 4}", dma=True)
            em.wait(f"te_r1_{b}_31")
            if b == 0:
                em.wait("cv_wo")
            em.op(lambda: s.dma_start(
                out=S["wo"][:].rearrange("p (c n) -> p c n", c=DCH),
                in_=AP(dwob.tensor, 0, [[D, 128], [128 * D, DCH], [1, D]])),
                mark=f"wd_wo{b}", sem="swo", dma=True)


        em.wait("ve_y1_done")
        em.wait("gp_acc31")
        if _DEBUG:
            em.wait("sst_dbgy2")
        for gi in range(NGRP):
            for e in range(NE):
                if gi == 0:
                    em.wait(f"cv_w1_{e}")
                for pc in range(0, 16, 2):
                    if not (gi == 0 and e == 0 and pc < 4):
                        if pc >= 4:
                            em.wait(f"te_a{gi}_{e}_{pc * 2 - 7}")
                        else:
                            pe, pg = (e - 1, gi) if e > 0 else (NE - 1, gi - 1)
                            em.wait(f"te_a{pg}_{pe}_{pc * 2 + 25}")
                    em.op(lambda e=e, pc=pc: s.dma_start(
                        out=S[f"w1c{pc % 4}"][:].rearrange("p (c f) -> p c f", c=DCH),
                        in_=AP(dw1b.tensor, e * D * FF + pc * 256,
                               [[FF, 128], [128 * FF, DCH], [1, 256]])),
                        mark=f"wd_w1_{gi}_{e}_{pc}", sem=f"sw1{pc % 4}", dma=True)

    bodies = [body_gpsimd, body_sync, body_tensor, body_vector, body_scalar]

    class _Mock:
        def wait_ge(self, *a, **k):
            pass

    em.start_pass(1)
    em.cur_engine = _Mock()
    mock = em.cur_engine
    for f in bodies:
        em.cur_engine = mock
        f(None)

    em.start_pass(2)
    from contextlib import ExitStack
    with ExitStack() as ctx:
        for sn in SEMNAMES:
            em.sems[sn] = ctx.enter_context(nc.semaphore(sn))
        block = ctx.enter_context(nc.Block())
        block.gpsimd(lambda g: body_gpsimd(g))
        block.sync(lambda sy: body_sync(sy))
        block.tensor(lambda t: body_tensor(t))
        block.vector(lambda v: body_vector(v))
        block.scalar(lambda s: body_scalar(s))
    return nc


_NC = None


def kernel(L0_attn_x, L0_attn_W_in, L0_attn_b_in, L0_attn_k0, L0_attn_k1,
           L0_attn_gamma, L0_attn_beta, L0_attn_W_out, L0_attn_b_out,
           L0_ffn_router_w, L0_ffn_w1, L0_ffn_b1, L0_ffn_w2, L0_ffn_b2):
    global _NC
    x = np.ascontiguousarray(np.asarray(L0_attn_x, dtype=np.float32))
    w_in = np.ascontiguousarray(np.asarray(L0_attn_W_in, dtype=np.float32))
    b_in = np.ascontiguousarray(np.asarray(L0_attn_b_in, dtype=np.float32))
    k0 = np.ascontiguousarray(np.asarray(L0_attn_k0, dtype=np.float32)[0])
    k1 = np.ascontiguousarray(np.asarray(L0_attn_k1, dtype=np.float32)[0])
    gamma = np.ascontiguousarray(np.asarray(L0_attn_gamma, dtype=np.float32))
    beta = np.ascontiguousarray(np.asarray(L0_attn_beta, dtype=np.float32))
    w_out = np.ascontiguousarray(np.asarray(L0_attn_W_out, dtype=np.float32))
    b_out = np.ascontiguousarray(np.asarray(L0_attn_b_out, dtype=np.float32))
    rw = np.ascontiguousarray(np.asarray(L0_ffn_router_w, dtype=np.float32).T)
    w1t = np.ascontiguousarray(np.asarray(L0_ffn_w1, dtype=np.float32).transpose(0, 2, 1))
    b1 = np.ascontiguousarray(np.asarray(L0_ffn_b1, dtype=np.float32))
    w2t = np.ascontiguousarray(np.asarray(L0_ffn_w2, dtype=np.float32).transpose(0, 2, 1))
    b2 = np.ascontiguousarray(np.asarray(L0_ffn_b2, dtype=np.float32))

    if _NC is None:
        _NC = _build_nc()

    in_maps = []
    for c in range(8):
        xs = np.ascontiguousarray(x[2 * c: 2 * c + 2].reshape(TTOT, D))
        in_maps.append(dict(x=xs, w_in=w_in, b_in=b_in, k0=k0, k1=k1,
                            gamma=gamma, beta=beta, w_out=w_out, b_out=b_out,
                            rw=rw, w1t=w1t, b1=b1, w2t=w2t, b2=b2))
    res = run_bass_kernel_spmd(_NC, in_maps, core_ids=list(range(8)))
    outs = [np.asarray(res.results[c]["out"]).reshape(2, T, D) for c in range(8)]
    return np.concatenate(outs, axis=0)


# version 13
# speedup vs baseline: 1.1261x; 1.1261x over previous
import numpy as np
import concourse.bass as bass
import concourse.mybir as mybir
from concourse.bass_types import AP

from concourse.bass_utils import run_bass_kernel_spmd

F32 = mybir.dt.float32
BF16 = mybir.dt.bfloat16
AF = mybir.ActivationFunctionType
ALU = mybir.AluOpType
AX = mybir.AxisListType
I32 = mybir.dt.int32

_DEBUG = False

D = 1024
DCH = 8
T = 2048
B_LOC = 2
NT = 16
NQ = B_LOC * NT
PROJ = 3072
FF = 4096
NE = 8
TGRP = 1024
NGRP = 4
CAP = 384
NST = CAP // 128
KS = 128
EPS = 1e-5
TTOT = NQ * 128
_SBUF_BASE = 16512

KB = 1024
OFF = {}
_cur = 0
def _lay(name, size, at=None):
    global _cur
    if at is not None:
        _cur = at
    OFF[name] = _cur
    _cur += size

_lay("ident", 256)
_lay("rev", 256)
_lay("ones", 1024)
_lay("coef", 1024)
_lay("scr", 128)
_lay("scr2", 128)
_lay("statp", 64)
_lay("strows", 256)
_lay("rt8", 64)
_lay("gcol", 64)
_lay("bcol", 64)
_lay("rw", 128)
assert _cur <= 4 * KB
_lay("bo", 2 * KB, at=4 * KB)
_lay("bi", 6 * KB)
_lay("srow", 6 * KB)
_lay("srtmp", 1 * KB)
assert _cur <= 20 * KB
_lay("tmp16", 12 * KB, at=172 * KB)
_lay("x_in", 8 * KB, at=20 * KB)
_lay("xn32", 8 * KB)
_lay("xnbf", 4 * KB)
_lay("xnt", 4 * KB)
_lay("vtmp", 4 * KB)
_lay("xsq", 2 * KB)
_lay("accp", 8 * KB)
assert _cur <= 60 * KB
_lay("y1tm", 32 * KB, at=20 * KB)
_lay("wo", 16 * KB, at=20 * KB)
_lay("y2t", 4 * KB)
_lay("accin", 8 * KB)
_lay("accout", 8 * KB)
assert _cur <= 60 * KB
_lay("wp", 48 * KB, at=60 * KB)
_lay("toep", 32 * KB, at=60 * KB)
_lay("vrev", 32 * KB, at=108 * KB)
_lay("y2tm", 32 * KB, at=108 * KB)
_lay("x1tm", 32 * KB, at=140 * KB)
_lay("y1rev", 32 * KB, at=140 * KB)
_lay("x2tm", 32 * KB, at=172 * KB)
assert _cur <= 206 * KB
_lay("ltri", 512, at=4 * KB)
_lay("onesf", 512)
_lay("iot", 1536)
_lay("maskm", 256)
_lay("posd", 512)
_lay("tta", 768)
_lay("rowbuf", 1 * KB)
_lay("coefbf", 512)
_lay("gcoefc", 128)
_lay("xnbg", 16 * KB, at=10 * KB)
_lay("w1c", 16 * KB, at=26 * KB)
_lay("b2r", 4 * KB, at=42 * KB)
_lay("b1call", 1 * KB)
_lay("b1r", 8 * KB, at=188 * KB)
_lay("gt", 12 * KB, at=48 * KB)
_lay("gs", 12 * KB, at=60 * KB)
_lay("ysb", 12 * KB, at=72 * KB)
_lay("xgsb", 4 * KB, at=84 * KB)
_lay("xgt", 12 * KB, at=88 * KB)
_lay("hT", 24 * KB, at=100 * KB)
_lay("w2c", 32 * KB, at=124 * KB)
_lay("ygrp", 32 * KB, at=156 * KB)
assert _cur <= 206 * KB, _cur


class _Em:

    def __init__(self):
        self.marks = {}
        self.counts = {}
        self.passno = 1
        self.sems = {}
        self.cur_engine = None

    def start_pass(self, n):
        self.passno = n
        self.counts = {}

    def op(self, f, mark=None, sem=None, dma=False):
        if mark is not None:
            assert sem is not None, mark
            inc = 16 if dma else 1
            self.counts[sem] = self.counts.get(sem, 0) + inc
            if self.passno == 1:
                assert mark not in self.marks, f"dup mark {mark}"
                self.marks[mark] = (sem, self.counts[sem])
                return None
            instr = f()
            instr.then_inc(self.sems[sem], inc)
            return instr
        if self.passno == 1:
            return None
        return f()

    def wait(self, label):
        if self.passno == 1:
            assert isinstance(label, str)
            return
        semname, val = self.marks[label]
        self.cur_engine.wait_ge(self.sems[semname], val)

    def wait_total(self, semname):
        val = self.counts.get(semname, 0)
        if self.passno == 1:
            return
        self.cur_engine.wait_ge(self.sems[semname], val)


def _build_nc():
    nc = bass.Bass("TRN2", target_bir_lowering=False)

    dx = nc.dram_tensor("x", [TTOT, D], F32, kind="ExternalInput").ap()
    dwin = nc.dram_tensor("w_in", [D, PROJ], F32, kind="ExternalInput").ap()
    dbin = nc.dram_tensor("b_in", [PROJ], F32, kind="ExternalInput").ap()
    dk0 = nc.dram_tensor("k0", [D, KS], F32, kind="ExternalInput").ap()
    dk1 = nc.dram_tensor("k1", [D, KS], F32, kind="ExternalInput").ap()
    dgam = nc.dram_tensor("gamma", [D], F32, kind="ExternalInput").ap()
    dbet = nc.dram_tensor("beta", [D], F32, kind="ExternalInput").ap()
    dwout = nc.dram_tensor("w_out", [D, D], F32, kind="ExternalInput").ap()
    dbout = nc.dram_tensor("b_out", [D], F32, kind="ExternalInput").ap()
    drw = nc.dram_tensor("rw", [D, NE], F32, kind="ExternalInput").ap()
    dw1 = nc.dram_tensor("w1t", [NE, D, FF], F32, kind="ExternalInput").ap()
    db1 = nc.dram_tensor("b1", [NE, FF], F32, kind="ExternalInput").ap()
    dw2 = nc.dram_tensor("w2t", [NE, FF, D], F32, kind="ExternalInput").ap()
    db2 = nc.dram_tensor("b2", [NE, D], F32, kind="ExternalInput").ap()
    dout = nc.dram_tensor("out", [TTOT, D], F32, kind="ExternalOutput").ap()

    ikind = "ExternalOutput" if _DEBUG else "Internal"
    dacc = nc.dram_tensor("dbg_acc", [TTOT, D], F32, kind=ikind).ap()
    dxnb = nc.dram_tensor("dbg_xnb", [TTOT, D], BF16, kind=ikind).ap()
    dwall = nc.dram_tensor("wall", [D, 768], BF16, kind="Internal").ap()
    dtoep = nc.dram_tensor("dtoep", [128, 128 * 4096], BF16, kind="Internal").ap()
    dwob = nc.dram_tensor("wob", [D, D], BF16, kind="Internal").ap()
    dw1b = nc.dram_tensor("w1b", [NE, D, FF], BF16, kind="Internal").ap()
    if _DEBUG:
        dcoef = nc.dram_tensor("dbg_coef", [128, NQ * 8], F32, kind=ikind).ap()
        dvrev = nc.dram_tensor("dbg_vrev", [128, NT * D], BF16, kind=ikind).ap()
        dy2 = nc.dram_tensor("dbg_y2", [128, NT * D], BF16, kind=ikind).ap()

    arena = nc.alloc_sbuf_tensor_at(
        "arena", [128, 206 * KB // 2], BF16, offset=_SBUF_BASE).ap()

    S = {}
    def sba(key, shape, dt, off_name=None, extra=0):
        off_name = off_name or key
        off = OFF[off_name] + extra
        esz = 2 if dt == BF16 else 4
        nbf16 = shape[1] * esz // 2
        S[key] = arena[0:shape[0], off // 2: off // 2 + nbf16].bitcast(dt)

    sba("ident", [128, 128], BF16)
    sba("rev", [128, 128], BF16)
    sba("ones", [128, 512], BF16)
    sba("coef", [128, 256], F32)
    sba("scr", [128, 32], F32)
    sba("scr2", [128, 32], F32)
    sba("statp", [128, 2], BF16)
    sba("strows", [2, 128], BF16)
    sba("rt8", [128, 8], F32)
    sba("gcol", [128, 8], F32)
    sba("bcol", [128, 8], BF16)
    sba("rw", [128, 64], BF16)
    sba("bo", [1, 1024], BF16)
    sba("bi", [1, PROJ], BF16)
    sba("srow", [2, PROJ], BF16)
    sba("srtmp", [1, 512], BF16)
    for p in range(2):
        sba(f"tmp16_{p}", [128, PROJ], BF16, "tmp16", p * 6 * KB)
        sba(f"x_in{p}", [128, 1024], F32, "x_in", p * 4 * KB)
        sba(f"xn32{p}", [128, 1024], F32, "xn32", p * 4 * KB)
        sba(f"xnbf{p}", [128, 1024], BF16, "xnbf", p * 2 * KB)
        sba(f"xnt{p}", [128, 1024], BF16, "xnt", p * 2 * KB)
        sba(f"vtmp{p}", [128, 1024], BF16, "vtmp", p * 2 * KB)
        sba(f"accp{p}", [128, 1024], F32, "accp", p * 4 * KB)
        sba(f"y2t{p}", [128, 1024], BF16, "y2t", p * 2 * KB)
        sba(f"accin{p}", [128, 1024], F32, "accin", p * 4 * KB)
        sba(f"accout{p}", [128, 1024], F32, "accout", p * 4 * KB)
    sba("xsq", [128, 1024], BF16)
    sba("y1tm", [128, NT * D], BF16)
    sba("wo", [128, DCH * D], BF16)
    sba("wp", [128, DCH * PROJ], BF16)
    for p in range(4):
        sba(f"toep{p}", [128, 32 * 128], BF16, "toep", p * 8 * KB)
    sba("vrev", [128, NT * D], BF16)
    sba("y2tm", [128, NT * D], BF16)
    sba("x1tm", [128, NT * D], BF16)
    sba("y1rev", [128, NT * D], BF16)
    sba("x2tm", [128, NT * D], BF16)
    sba("b1r", [8, FF], BF16)
    for p in range(4):
        sba(f"w1c{p}", [128, DCH * 256], BF16, "w1c", p * 4 * KB)
    for p in range(2):
        sba(f"b2r{p}", [1, D], BF16, "b2r", p * 2 * KB)
    sba("b1call", [128, 256], F32)
    sba("ltri", [128, 128], F32)
    sba("onesf", [128, 128], F32)
    sba("iot", [128, 384], F32)
    sba("maskm", [128, 64], F32)
    for p in range(2):
        sba(f"posd{p}", [128, 64], F32, "posd", p * 256)
        sba(f"gcoefc{p}", [128, 3], F32, "gcoefc", p * 64)
        sba(f"gt{p}", [128, 8 * 384], BF16, "gt", p * 6 * KB)
        sba(f"gs{p}", [128, 3 * 1024], BF16, "gs", p * 6 * KB)
        sba(f"ysb{p}", [128, 3 * 1024], BF16, "ysb", p * 6 * KB)
        sba(f"xgsb{p}", [128, 1024], BF16, "xgsb", p * 2 * KB)
        sba(f"xgt{p}", [128, DCH * CAP], BF16, "xgt", p * 6 * KB)
    for nm in ("tt0", "tt1", "tt2"):
        sba(nm, [128, 64], F32, "tta", ("tt0", "tt1", "tt2").index(nm) * 256)
    for p in range(4):
        sba(f"row{p}", [1, 64], F32, "rowbuf", p * 256)
        sba(f"w2c{p}", [128, 4 * 1024], BF16, "w2c", p * 8 * KB)
    sba("coefbf", [128, 256], BF16)
    sba("xnbg", [128, 8 * 1024], BF16)
    sba("hT", [128, 32 * CAP], BF16)
    sba("ygrp", [128, 8 * 1024], F32)

    PB = [nc.alloc_psum_tensor(f"pb{i}", [128, 512], F32).ap() for i in range(7)]
    pb7 = nc.alloc_psum_tensor("pb7", [128, 1024], BF16).ap()
    pb7f = pb7[:].bitcast(F32)

    em = _Em()
    SEMNAMES = (["s0", "sxg", "syg", "ssr", "sout", "swo", "sb1", "sdb",
                "stx0", "stx1", "scwo", "stp3g",
                "ste", "sve", "ssc", "sgp"]
                + [f"sx{p}" for p in range(2)] + [f"si{p}" for p in range(2)]
                + [f"sp{p}" for p in range(2)] + [f"sn{p}" for p in range(2)]
                + [f"sw{p}" for p in range(2)] + [f"swi{p}" for p in range(2)]
                + [f"stp{p}" for p in range(4)]
                + [f"sw1{p}" for p in range(4)] + [f"sw2{p}" for p in range(4)]
                + [f"scv1_{e}" for e in range(NE)]
                + [f"sb2{p}" for p in range(2)])

    GBANK = lambda sb: (PB[2 * (sb % 2)], PB[2 * (sb % 2) + 1])
    BCH = [2, 3, 4, 5, 6, 7]

    def wall_src(cv, chunk, cg):
        off = cv * 384 + (129 if chunk == 0 else 1) + (cg * 32) * 768
        return AP(dwall.tensor, off, [[1, 128], [768, 32], [1, 128]])

    def rearr(t):
        return t[:].rearrange("p (j d) -> p d j", d=1024)

    qlast = lambda b: b * NT + NT - 1

    def body_gpsimd(g):
        em.cur_engine = g
        em.op(lambda: g.memset(S["ident"][:], 0.0), mark="gp_m1", sem="sgp")
        em.wait("gp_m1")
        em.op(lambda: g.affine_select(out=S["ident"][:], in_=S["ident"][:],
                                      compare_op=ALU.not_equal, fill=1.0, base=0,
                                      pattern=[[-1, 128]], channel_multiplier=1),
              mark="gp_m2", sem="sgp")
        em.op(lambda: g.memset(S["rev"][:], 0.0), mark="gp_m3", sem="sgp")
        em.wait("gp_m3")
        em.op(lambda: g.affine_select(out=S["rev"][:], in_=S["rev"][:],
                                      compare_op=ALU.not_equal, fill=1.0, base=-127,
                                      pattern=[[1, 128]], channel_multiplier=1),
              mark="gp_const", sem="sgp")
        for cc in range(DCH):
            em.op(lambda cc=cc: g.dma_start(
                out=S["gcol"][:, cc:cc + 1], in_=AP(dgam.tensor, cc * 128, [[1, 128]])),
                mark=f"wd_gcol{cc}", sem="s0", dma=True)
            em.op(lambda cc=cc: g.dma_start(
                out=S["bcol"][:, cc:cc + 1], in_=AP(dbet.tensor, cc * 128, [[1, 128]])),
                mark=f"wd_bcol{cc}", sem="s0", dma=True)
        em.op(lambda: g.dma_start(
            out=S["rw"][:].rearrange("p (c e) -> p c e", c=DCH),
            in_=AP(drw.tensor, 0, [[8, 128], [1024, DCH], [1, 8]])),
            mark="wd_rw", sem="s0", dma=True)
        em.op(lambda: g.dma_start(
            out=S["bo"][0:1, :], in_=AP(dbout.tensor, 0, [[1024, 1], [1, 1024]])),
            mark="wd_bo", sem="s0", dma=True)
        em.op(lambda: g.dma_start(
            out=S["bi"][0:1, :], in_=AP(dbin.tensor, 0, [[PROJ, 1], [1, PROJ]])),
            mark="wd_bi", sem="s0", dma=True)
        em.wait("ve_zrow")
        for r in range(8):
            em.op(lambda r=r: g.dma_start(
                out=AP(dwall.tensor, r * 128 * 768, [[768, 128], [1, 768]]),
                in_=S["xsq"][:, 0:768]),
                mark=f"wd_wz{r}", sem="s0", dma=True)
        em.wait("wd_wz7")
        em.op(lambda: g.dma_start(
            out=AP(dwall.tensor, 128, [[768, D], [1, KS]]), in_=dk0),
            mark="wd_k0", sem="s0", dma=True)
        em.op(lambda: g.dma_start(
            out=AP(dwall.tensor, 384 + 128, [[768, D], [1, KS]]), in_=dk1),
            mark="wd_k1", sem="s0", dma=True)

        for b in range(B_LOC):
            for cc in range(DCH):
                if b == 1 and cc == 0:
                    em.wait("ve_g0_1_31")
                if cc >= 2:
                    em.wait(f"ve_wp{b}_{cc - 2}")
                    if b == 0:
                        em.wait(f"te_biasc{cc - 2}")
                em.op(lambda cc=cc: g.dma_start(
                    out=S[f"tmp16_{cc % 2}"][:],
                    in_=AP(dwin.tensor, cc * 128 * PROJ, [[PROJ, 128], [1, PROJ]])),
                    mark=f"wd_winc{b}_{cc}", sem=f"swi{cc % 2}", dma=True)
            if b == 0:
                em.op(lambda: g.dma_start(out=dwob, in_=dwout),
                      mark="cv_wo", sem="scwo", dma=True)
                for cv in range(2):
                    for cg in range(32):
                        for ch in range(2):
                            idx = (cv * 2 + ch) * 32 + cg
                            last = (cg == 31 and ch == 1)
                            em.op(lambda cv=cv, ch=ch, cg=cg, idx=idx: g.dma_start(
                                out=AP(dtoep.tensor, idx * 128 * 4096,
                                       [[4096, 128], [128, 32], [1, 128]]),
                                in_=wall_src(cv, ch, cg)),
                                mark=(f"gp_txe{cv}" if last else f"gp_tx{cv}_{ch}_{cg}"),
                                sem=f"stx{cv}", dma=True)
            if b == 0:
                for e in range(NE):
                    em.op(lambda e=e: g.dma_start(
                        out=AP(dw1b.tensor, e * D * FF, [[FF, D], [1, FF]]),
                        in_=AP(dw1.tensor, e * D * FF, [[FF, D], [1, FF]])),
                        mark=f"cv_w1_{e}", sem=f"scv1_{e}", dma=True)
            if b == 1:
                for cv in range(2):
                    for cg in range(1, 32, 2):
                        ch = 1
                        prev = cg * 2 + ch - 4
                        if prev >= 0:
                            em.wait(f"te_cv1_{cv}_{prev // 2}")
                        elif cv == 1:
                            em.wait(f"te_cv1_0_{(64 + prev) // 2}")
                        else:
                            em.wait("te_proj1_15_5")
                        em.op(lambda cv=cv, ch=ch, cg=cg: g.dma_start(
                            out=S[f"toep{(cg * 2 + ch) % 4}"][:],
                            in_=AP(dtoep.tensor,
                                   ((cv * 2 + ch) * 32 + cg) * 128 * 4096,
                                   [[4096, 128], [1, 4096]])),
                            mark=f"wd_tp1_{cv}_{cg}_{ch}",
                            sem="stp3g", dma=True)



        em.wait("ve_y1_done")
        em.wait("sst_xnt30")
        em.wait("sst_xnt31")
        if _DEBUG:
            em.wait("sst_dbgy2")
        em.op(lambda: g.dma_start(out=S["b1r"][0:8, :], in_=db1),
              mark="sb1_all", sem="sb1", dma=True)
        for e0 in range(2):
            em.op(lambda e0=e0: g.dma_start(
                out=S[f"b2r{e0 % 2}"][0:1, :],
                in_=AP(db2.tensor, e0 * D, [[D, 1], [1, D]])),
                mark=f"wd_b2_0_{e0}", sem=f"sb2{e0 % 2}", dma=True)
        em.op(lambda: g.memset(S["ltri"][:], 0.0), mark="gp_p2a", sem="sgp")
        em.wait("gp_p2a")
        em.op(lambda: g.affine_select(out=S["ltri"][:], in_=S["ltri"][:],
                                      compare_op=ALU.is_ge, fill=1.0, base=0,
                                      pattern=[[-1, 128]], channel_multiplier=1))
        em.op(lambda: g.memset(S["onesf"][:], 1.0))
        em.op(lambda: g.iota(S["iot"][:], pattern=[[1, 384]], base=0,
                             channel_multiplier=0,
                             allow_small_or_imprecise_dtypes=True),
              mark="gp_p2c", sem="sgp")
        for gi in range(NGRP):
            for e in range(NE):
                ge = gi * NE + e
                for k in range(8):
                    if ge == 0 and k < 4:
                        em.wait("te_y2tr1_15")
                        if _DEBUG:
                            em.wait("sst_dbgy2")
                    elif k < 4:
                        pgi2, pe2 = divmod(ge - 1, NE)
                        phx = 4 * k + 19
                        em.wait(f"te_bphi{pgi2}_{pe2}_{phx}" if phx < 31
                                else f"te_b{pgi2}_{pe2}_5")
                    else:
                        em.wait(f"te_bphi{gi}_{e}_{4 * (k - 4) + 3}")
                    em.op(lambda e=e, k=k: g.dma_start(
                        out=S[f"w2c{k % 4}"][:].rearrange("p (j c) -> p j c", j=4),
                        in_=AP(dw2.tensor, e * FF * D + k * 4 * 128 * D,
                               [[D, 128], [128 * D, 4], [1, D]])),
                        mark=f"wd_w2_{gi}_{e}_{k}", sem=f"sw2{k % 4}", dma=True)
                ge2 = ge + 2
                if ge2 < NGRP * NE:
                    g2_, e2 = divmod(ge2, NE)
                    em.wait(f"te_bphi{gi}_{e}_0")
                    em.op(lambda e2=e2: g.dma_start(
                        out=S[f"b2r{e2 % 2}"][0:1, :],
                        in_=AP(db2.tensor, e2 * D, [[D, 1], [1, D]])),
                        mark=f"wd_b2_{g2_}_{e2}", sem=f"sb2{e2 % 2}", dma=True)

    def body_sync(sy):
        em.cur_engine = sy
        for ng in range(6):
            em.wait(f"ve_srow_bias{ng}")
            em.op(lambda ng=ng: sy.dma_start(
                out=S["srow"][1:2, ng * 512:(ng + 1) * 512], in_=S["srtmp"][0:1, :]),
                mark=f"sst_srow_r1_{ng}", sem="ssr", dma=True)

        for b in range(B_LOC):
            def x_load(j):
                q = b * NT + j
                if j < 2 and b == 1:
                    em.wait("ve_wev0_15_1")
                    em.wait_total("sw0")
                    em.wait_total("sw1")
                    em.wait("te_w0_15_1")
                if q >= 2:
                    em.wait(f"ve_xnbf{q - 2}")
                    em.wait(f"sc_sq2_{q - 2}")
                    em.wait(f"gp_acc{q - 2}")
                em.op(lambda q=q: sy.dma_start(
                    out=S[f"x_in{q % 2}"][:],
                    in_=AP(dx.tensor, q * 128 * D, [[D, 128], [1, D]])),
                    mark=f"sx_x{q}", sem=f"sx{q % 2}", dma=True)

            def p_stores(j):
                q = b * NT + j
                em.wait(f"gp_acc{q}")
                em.op(lambda q=q: sy.dma_start(
                    out=AP(dacc.tensor, q * 128 * D, [[D, 128], [1, D]]),
                    in_=S[f"accp{q % 2}"][:]),
                    mark=f"sst_acc{q}", sem=f"sp{q % 2}", dma=True)
                em.wait(f"ve_xnbf{q}")
                em.op(lambda q=q: sy.dma_start(
                    out=AP(dxnb.tensor, q * 128 * D, [[D, 128], [1, D]]),
                    in_=S[f"xnbf{q % 2}"][:]),
                    mark=f"sst_xnt{q}", sem=f"sn{q % 2}", dma=True)

            for j in range(NT):
                x_load(j)
                if j >= 1:
                    p_stores(j - 1)
            p_stores(NT - 1)

            for cv in range(2):
                for cg in (range(32) if b == 0 else range(0, 32, 2)):
                    for ch in (1,):
                        prev = cg * 2 + ch - 4
                        if prev >= 0:
                            em.wait(f"te_cv{b}_{cv}_{prev // 2}")
                        elif cv == 1:
                            em.wait(f"te_cv{b}_0_{(64 + prev) // 2}")
                        else:
                            em.wait("wd_k1")
                            em.wait(f"te_proj{b}_15_5")
                        em.wait(f"gp_txe{cv}")
                        em.op(lambda cv=cv, ch=ch, cg=cg,
                              slot=(cg * 2 + ch) % 4: sy.dma_start(
                            out=S[f"toep{slot}"][:],
                            in_=AP(dtoep.tensor,
                                   ((cv * 2 + ch) * 32 + cg) * 128 * 4096,
                                   [[4096, 128], [1, 4096]])),
                            mark=f"wd_tp{b}_{cv}_{cg}_{ch}",
                            sem=f"stp{(cg * 2 + ch) % 4}", dma=True)

            def acc_load(j):
                q = b * NT + j
                em.wait_total("sp0")
                em.wait_total("sp1")
                if j < 2:
                    em.wait(f"te_r1_{b}_31")
                    em.wait(f"sst_xnt{qlast(b)}")
                else:
                    em.wait(f"ve_wev{b}_{j - 2}_1")
                em.op(lambda q=q, j=j: sy.dma_start(
                    out=S[f"accin{j % 2}"][:],
                    in_=AP(dacc.tensor, q * 128 * D, [[D, 128], [1, D]])),
                    mark=f"sx_accin{q}", sem=f"si{j % 2}", dma=True)

            def acc_store(j):
                q = b * NT + j
                em.wait(f"ve_wev{b}_{j}_1")
                em.op(lambda q=q, j=j: sy.dma_start(
                    out=AP(dacc.tensor, q * 128 * D, [[D, 128], [1, D]]),
                    in_=S[f"accout{j % 2}"][:]),
                    mark=f"sst_accw{q}", sem=f"sw{j % 2}", dma=True)

            for j in range(NT):
                acc_load(j)
                if j >= 1:
                    acc_store(j - 1)
            acc_store(NT - 1)

        if _DEBUG:
            em.wait("ve_y1_done")
            em.op(lambda: sy.dma_start(out=dcoef, in_=S["coef"][:]),
                  mark="sst_dbgcoef", sem="sdb", dma=True)
            em.op(lambda: sy.dma_start(out=dvrev, in_=S["vrev"][:]),
                  mark="sst_dbgvrev", sem="sdb", dma=True)
            em.op(lambda: sy.dma_start(out=dy2, in_=S["y2tm"][:]),
                  mark="sst_dbgy2", sem="sdb", dma=True)

        for gi in range(NGRP):
            if gi == 0:
                em.wait("ve_y1_done")
                em.wait("sc_sq2_31")
                em.wait("gp_acc31")
                em.wait("te_proj1_15_5")
                em.wait("sst_xnt30")
                em.wait("sst_xnt31")
            else:
                em.wait(f"te_gmm{gi - 1}_{NE - 1}_{NST - 1}")
            em.op(lambda gi=gi: sy.dma_start(
                out=S["xnbg"][:].rearrange("p (t c) -> p t c", t=8),
                in_=AP(dxnb.tensor, gi * 8 * 128 * D, [[D, 128], [128 * D, 8], [1, D]])),
                mark=f"wd_xnbg{gi}", sem="sxg", dma=True)
            if gi == 0:
                em.wait("te_cv1_1_31")
            else:
                em.wait(f"sout_{gi - 1}")
            em.wait(f"sst_accw{gi * 8 + 6}")
            em.wait(f"sst_accw{gi * 8 + 7}")
            em.op(lambda gi=gi: sy.dma_start(
                out=S["ygrp"][:].rearrange("p (t c) -> p t c", t=8),
                in_=AP(dacc.tensor, gi * 8 * 128 * D, [[D, 128], [128 * D, 8], [1, D]])),
                mark=f"wd_yg{gi}", sem="syg", dma=True)
            for e in range(NE):
                ge = gi * NE + e
                if gi == 0:
                    em.wait(f"cv_w1_{e}")
                for pc in range(1, 16, 2):
                    if not (ge == 0 and pc < 4):
                        if pc >= 4:
                            em.wait(f"te_a{gi}_{e}_{pc * 2 - 7}")
                        else:
                            pe2, pg2 = (e - 1, gi) if e > 0 else (NE - 1, gi - 1)
                            em.wait(f"te_a{pg2}_{pe2}_{pc * 2 + 25}")
                    em.op(lambda e=e, pc=pc: sy.dma_start(
                        out=S[f"w1c{pc % 4}"][:].rearrange("p (c f) -> p c f", c=DCH),
                        in_=AP(dw1b.tensor, e * D * FF + pc * 256,
                               [[FF, 128], [128 * FF, DCH], [1, 256]])),
                        mark=f"wd_w1_{gi}_{e}_{pc}", sem=f"sw1{pc % 4}", dma=True)
            em.wait(f"ve_ysum{gi}_{NE - 1}_7_1")
            em.op(lambda gi=gi: sy.dma_start(
                out=AP(dout.tensor, gi * 8 * 128 * D, [[D, 128], [128 * D, 8], [1, D]]),
                in_=S["ygrp"][:].rearrange("p (t c) -> p t c", t=8)),
                mark=f"sout_{gi}", sem="sout", dma=True)
        em.wait_total("sout")
        for sn_ in ("sp0", "sp1", "sn0", "sn1", "sw0", "sw1", "ssr", "sdb",
                    "sxg", "syg"):
            em.wait_total(sn_)

    def body_tensor(t):
        em.cur_engine = t

        def mm(out, lhsT, rhs, start, stop, mark=None, sem="ste"):
            em.op(lambda: t.matmul(out, lhsT, rhs, start=start, stop=stop,
                                   skip_group_check=True),
                  mark=mark, sem=sem)

        em.wait("gp_const")
        em.wait("ve_ones")
        for b in range(B_LOC):
            if b == 0:
                for cc in range(DCH):
                    em.wait(f"ve_wp0_{cc}")
                    em.wait("wd_k1")
                    for ng in range(6):
                        mm(PB[ng][0:1, :], S["bcol"][:, cc:cc + 1],
                           S[f"tmp16_{cc % 2}"][:, ng * 512:(ng + 1) * 512],
                           start=(cc == 0), stop=False,
                           mark=(f"te_biasc{cc}" if ng == 5 else None), sem="ste")
                for ng in range(6):
                    mm(PB[ng][0:1, :], S["ones"][0:1, 0:1],
                       S["bi"][0:1, ng * 512:(ng + 1) * 512],
                       start=False, stop=True, mark=f"te_bias{ng}", sem="ste")
                for ng in range(6):
                    em.wait(f"ve_srow_bias{ng}")
                    for cc in range(DCH):
                        mm(PB[ng][0:1, :], S["ones"][:, 0:1],
                           S["wp"][:, cc * PROJ + ng * 512: cc * PROJ + (ng + 1) * 512],
                           start=(cc == 0), stop=(cc == DCH - 1),
                           mark=(f"te_sp{ng}" if cc == DCH - 1 else None), sem="ste")

            for j in range(NT):
                q = b * NT + j
                em.wait(f"ve_xnbf{q}")
                if q >= 1:
                    em.wait(f"sc_srt{q - 1}")
                if b == 1 and j == 0:
                    em.wait("ve_y2tev0_15")
                for cc in range(DCH):
                    em.op(lambda q=q, cc=cc: t.transpose(
                        pb7[:, cc * 128:(cc + 1) * 128],
                        S[f"xnbf{q % 2}"][:, cc * 128:(cc + 1) * 128],
                        S["ident"][:]),
                        mark=(f"te_tr{q}" if cc == DCH - 1 else None), sem="ste")
                em.wait(f"sc_xnt{q}")
                em.wait(f"ve_stat{q}")
                em.op(lambda: t.transpose(pb7[0:2, 0:128], S["statp"][:], S["ident"][:]),
                      mark=f"te_st{q}", sem="ste")
                em.wait(f"sc_srt{q}")
                if b == 0 and j == 0:
                    for ng in range(6):
                        em.wait(f"ve_srow_sp{ng}")
                    em.wait("sst_srow_r1_5")
                for ng in range(6):
                    bank = PB[ng % 2]
                    if q * 6 + ng >= 2:
                        pq, png = divmod(q * 6 + ng - 2, 6)
                        em.wait(f"ve_pevac{pq}_{png}")
                    for cc in range(DCH):
                        mm(bank, S[f"xnt{q % 2}"][:, cc * 128:(cc + 1) * 128],
                           S["wp"][:, cc * PROJ + ng * 512: cc * PROJ + (ng + 1) * 512],
                           start=(cc == 0), stop=False)
                    mm(bank, S["strows"][0:2, :],
                       S["srow"][:, ng * 512:(ng + 1) * 512],
                       start=False, stop=True, mark=f"te_proj{b}_{j}_{ng}", sem="ste")
                if q >= 1:
                    em.wait(f"ve_coef{q - 1}")
                for cc in range(DCH):
                    mm(PB[6][:, 0:8], S[f"xnt{q % 2}"][:, cc * 128:(cc + 1) * 128],
                       S["rw"][:, cc * 8:(cc + 1) * 8],
                       start=(cc == 0), stop=(cc == DCH - 1),
                       mark=(f"te_rt{q}" if cc == DCH - 1 else None), sem="ste")
                for h in range(2):
                    em.wait(f"ve_pevac{q}_{h}")
                    if q >= 1:
                        em.wait(f"sc_vrev{q - 1}_{h}")
                    elif b == 0:
                        pass
                    mm(PB[2 + h], S["rev"][:],
                       S[f"vtmp{q % 2}"][:, h * 512:(h + 1) * 512],
                       start=True, stop=True, mark=f"te_vrev{q}_{h}", sem="ste")

            for cv in range(2):
                src = S["vrev"] if cv == 0 else S["y1rev"]
                for cg in range(32):
                    bank = PB[cg % 2]
                    em.wait(f"wd_tp{b}_{cv}_{cg}_0")
                    em.wait(f"wd_tp{b}_{cv}_{cg}_1")
                    if cg == 0:
                        if cv == 0:
                            em.wait(f"ve_pevac{qlast(b)}_4")
                            em.wait(f"ve_pevac{qlast(b)}_5")
                            em.wait(f"sc_vrev{qlast(b)}_0")
                            em.wait(f"sc_vrev{qlast(b)}_1")
                        else:
                            em.wait(f"ve_r1ev_{b}_31")
                            em.wait(f"ve_g{b}_0_31")
                    if cg >= 2:
                        em.wait(f"ve_g{b}_{cv}_{cg - 2}")
                    for ch in range(32):
                        d = cg * 32 + ch
                        slot0 = (cg * 2) % 4
                        slot1 = (cg * 2 + 1) % 4
                        mm(bank[:, ch * 16: ch * 16 + 16],
                           S[f"toep{slot1}"][:, ch * 128:(ch + 1) * 128],
                           rearr(src)[:, d, 0:16],
                           start=(ch == 0), stop=False)
                        mm(bank[:, ch * 16 + 1: ch * 16 + 16],
                           S[f"toep{slot0}"][:, ch * 128:(ch + 1) * 128],
                           rearr(src)[:, d, 0:15],
                           start=False, stop=(ch == 31),
                           mark=(f"te_cv{b}_{cv}_{cg}" if ch == 31 else None), sem="ste")
                if cv == 0:
                    em.wait(f"ve_g{b}_0_31")
                    for h in range(32):
                        if h >= 2:
                            em.wait(f"ve_r1ev_{b}_{h - 2}")
                        else:
                            em.wait(f"sc_vrev{qlast(b)}_{h}")
                        mm(PB[2 + h % 2], S["rev"][:],
                           S["y1tm"][:, h * 512:(h + 1) * 512],
                           start=True, stop=True, mark=f"te_r1_{b}_{h}", sem="ste")

            em.wait(f"wd_wo{b}")
            em.wait(f"ve_g{b}_1_31")
            em.wait(f"sc_srt{qlast(b)}")
            for j in range(NT):
                q = b * NT + j
                if j >= 1:
                    em.wait(f"ve_y2tev{b}_{j - 1}")
                for cc in range(DCH):
                    em.op(lambda j=j, cc=cc: t.transpose(
                        pb7[:, cc * 128:(cc + 1) * 128],
                        S["y2tm"][:, j * 1024 + cc * 128: j * 1024 + (cc + 1) * 128],
                        S["ident"][:]),
                        mark=(f"te_y2tr{b}_{j}" if cc == DCH - 1 else None), sem="ste")
                em.wait(f"ve_y2tev{b}_{j}")
                for cg in range(2):
                    bank = PB[4 + cg]
                    if j >= 1:
                        em.wait(f"ve_wev{b}_{j - 1}_{cg}")
                    elif b == 1:
                        em.wait(f"ve_wev0_15_{cg}")
                    for cc in range(DCH):
                        mm(bank, S[f"y2t{j % 2}"][:, cc * 128:(cc + 1) * 128],
                           S["wo"][:, cc * D + cg * 512: cc * D + (cg + 1) * 512],
                           start=(cc == 0), stop=False)
                    mm(bank, S["ones"][0:1, 0:128],
                       S["bo"][0:1, cg * 512:(cg + 1) * 512],
                       start=False, stop=True, mark=f"te_w{b}_{j}_{cg}", sem="ste")

        em.wait("sb1_all")
        em.wait("ve_y2tev1_15")
        for phi in range(32):
            em.op(lambda phi=phi: t.transpose(
                pb7[:, phi * 8:(phi + 1) * 8],
                S["b1r"][0:8, phi * 128:(phi + 1) * 128], S["ident"][0:8, 0:8]),
                mark=("te_b1t" if phi == 31 else None), sem="ste")

        def mmf(out, lhsT, rhs, start, stop, mark=None):
            em.op(lambda: t.matmul(out, lhsT, rhs, start=start, stop=stop,
                                   skip_group_check=True), mark=mark, sem="ste")

        for gi in range(NGRP):
            em.wait(f"ve_mask{gi}")
            em.wait("gp_p2c")
            if gi == 0:
                em.wait("ve_b1c")
                em.wait("ve_cbf")
                em.wait("ve_coef31")
            else:
                em.wait(f"ve_ysb{gi - 1}_{NE - 1}_4")
            mmf(PB[6][0:1, 0:64], S["onesf"][:, 0:1], S["maskm"][:],
                start=True, stop=True, mark=f"te_cnt{gi}")
            em.wait(f"ve_cnt{gi}")
            em.wait(f"ve_base{gi}")
            mmf(PB[6][:, 0:64], S["ltri"][:], S["maskm"][:], start=True, stop=False)
            mmf(PB[6][:, 0:64], S["onesf"][0:1, 0:128], S["row3"][0:1, :],
                start=False, stop=True, mark=f"te_pos1{gi}")

            for e in range(NE):
                ge = gi * NE + e
                pgi, pe = divmod(ge - 1, NE)
                em.wait(f"ve_gt{gi}_{e}")
                if e == 0:
                    em.wait(f"wd_xnbg{gi}")
                if ge == 0:
                    em.wait("ve_g1_1_31")
                    em.wait("ve_r1ev_1_31")
                for sb in range(NST):
                    bk0, bk1 = GBANK(sb)
                    if sb == 2:
                        em.wait(f"ve_xgev{gi}_{e}_0")
                    elif ge > 0:
                        if sb == 0:
                            em.wait(f"ve_h{pgi}_{pe}_30")
                            em.wait(f"ve_ysum{pgi}_{pe}_6_0")
                        else:
                            em.wait(f"ve_ysum{pgi}_{pe}_6_1")
                            em.wait(f"ve_ysum{pgi}_{pe}_7_0")
                    for cg in range(2):
                        bk = bk0 if cg == 0 else bk1
                        for tt in range(8):
                            mmf(bk[:],
                                S[f"gt{e % 2}"][:, tt * 384 + sb * 128:
                                                tt * 384 + sb * 128 + 128],
                                S["xnbg"][:, tt * 1024 + cg * 512:
                                          tt * 1024 + (cg + 1) * 512],
                                start=(tt == 0), stop=(tt == 7))
                    if sb == 0 and ge > 0:
                        em.wait(f"ve_ysb{pgi}_{pe}_4")
                    for tt in range(8):
                        col = (gi * 8 + tt) * 8 + e
                        mmf(PB[6][:, 64 + sb:65 + sb],
                            S[f"gt{e % 2}"][:, tt * 384 + sb * 128:
                                            tt * 384 + sb * 128 + 128],
                            S["coefbf"][:, col:col + 1],
                            start=(tt == 0), stop=(tt == 7),
                            mark=(f"te_gmm{gi}_{e}_{sb}" if tt == 7 else None))
                for sb in range(NST):
                    em.wait(f"ve_xgev{gi}_{e}_{sb}")
                    if sb == 0:
                        if ge > 0:
                            em.wait(f"ve_ysb{pgi}_{pe}_5")
                    else:
                        em.wait(f"ve_xgtev{gi}_{e}_{sb - 1}")
                    for cc in range(DCH):
                        em.op(lambda sb=sb, cc=cc: t.transpose(
                            pb7[:, cc * 128:(cc + 1) * 128],
                            S[f"xgsb{sb % 2}"][:, cc * 128:(cc + 1) * 128],
                            S["ident"][:]),
                            mark=(f"te_xgtr{gi}_{e}_{sb}" if cc == DCH - 1 else None),
                            sem="ste")
                for phi in range(32):
                    em.wait(f"wd_w1_{gi}_{e}_{phi // 2}")
                    bank = PB[phi % 2]
                    if phi >= 2:
                        em.wait(f"ve_h{gi}_{e}_{phi - 2}")
                    else:
                        em.wait(f"ve_xgev{gi}_{e}_2")
                        if phi == 0:
                            em.wait(f"ve_xgtev{gi}_{e}_{NST - 1}")
                    for cc in range(DCH):
                        em.op(lambda phi=phi, cc=cc, e=e, bank=bank: t.matmul(
                            bank[:, 0:CAP],
                            S[f"w1c{(phi // 2) % 4}"][
                                :, cc * 256 + (phi % 2) * 128:
                                cc * 256 + (phi % 2) * 128 + 128],
                            S[f"xgt{e % 2}"][:, cc * CAP:(cc + 1) * CAP],
                            start=(cc == 0), stop=(cc == DCH - 1),
                            skip_group_check=True),
                            mark=(f"te_a{gi}_{e}_{phi}" if cc == DCH - 1 else None),
                            sem="ste")
                for sb in range(NST):
                    if sb == 0:
                        em.wait(f"ve_xgtev{gi}_{e}_2")
                    else:
                        em.wait(f"ve_gsev{gi}_{e}_{sb - 1}")
                    for tt in range(8):
                        em.op(lambda sb=sb, tt=tt, e=e: t.transpose(
                            pb7[:, tt * 128:(tt + 1) * 128],
                            S[f"gt{e % 2}"][:, tt * 384 + sb * 128:
                                            tt * 384 + sb * 128 + 128],
                            S["ident"][:]),
                            mark=(f"te_gstr{gi}_{e}_{sb}" if tt == 7 else None),
                            sem="ste")
                em.wait(f"wd_b2_{gi}_{e}")
                for ci, ch in enumerate(BCH):
                    if ci in (0, 1):
                        em.wait(f"ve_xgev{gi}_{e}_1")
                    elif ci == 2:
                        em.wait("ve_wev1_15_0" if ge == 0
                                else f"ve_ysum{pgi}_{pe}_7_1")
                    elif ci == 3:
                        em.wait("ve_wev1_15_1" if ge == 0
                                else f"ve_ysb{pgi}_{pe}_3")
                    elif ci == 4:
                        em.wait(f"ve_xgev{gi}_{e}_2")
                    else:
                        em.wait(f"ve_gsev{gi}_{e}_2")
                    em.op(lambda ci=ci, ch=ch, e=e: t.matmul(
                        (pb7f if ch == 7 else PB[ch])[:], S["ones"][0:1, 0:128],
                        S[f"b2r{e % 2}"][0:1, (ci % 2) * 512:(ci % 2) * 512 + 512],
                        start=True, stop=False, skip_group_check=True), sem="ste")
                for phi in range(32):
                    em.wait(f"ve_h{gi}_{e}_{phi}")
                    if phi % 4 == 0:
                        em.wait(f"wd_w2_{gi}_{e}_{phi // 4}")
                    for ci, ch in enumerate(BCH):
                        st, cg = ci // 2, ci % 2
                        if phi == 31:
                            mk = f"te_b{gi}_{e}_{ci}"
                        else:
                            mk = f"te_bphi{gi}_{e}_{phi}" if ci == 5 else None
                        em.op(lambda phi=phi, st=st, cg=cg, ch=ch: t.matmul(
                            (pb7f if ch == 7 else PB[ch])[:],
                            S["hT"][:, phi * CAP + st * 128:
                                    phi * CAP + st * 128 + 128],
                            S[f"w2c{(phi // 4) % 4}"][:, (phi % 4) * 1024 + cg * 512:
                                                      (phi % 4) * 1024 + (cg + 1) * 512],
                            start=False, stop=(phi == 31), skip_group_check=True),
                            mark=mk, sem="ste")
                em.wait(f"ve_gsev{gi}_{e}_2")
                for tt in range(8):
                    for cg in range(2):
                        bank = PB[1 + 2 * (tt % 2) + cg]
                        em.wait(f"ve_ysb{gi}_{e}_{4 + cg}")
                        if tt >= 2:
                            em.wait(f"ve_ysum{gi}_{e}_{tt - 2}_{cg}")
                        elif tt == 0 and cg == 0:
                            em.wait(f"ve_h{gi}_{e}_31")
                        elif tt == 0 and cg == 1:
                            em.wait(f"ve_ysb{gi}_{e}_0")
                        elif tt == 1 and cg == 0:
                            em.wait(f"ve_ysb{gi}_{e}_1")
                        else:
                            em.wait(f"ve_ysb{gi}_{e}_2")
                        for sb in range(NST):
                            em.op(lambda e=e, tt=tt, cg=cg, sb=sb, bank=bank:
                                  t.matmul(
                                      bank[:],
                                      S[f"gs{e % 2}"][:, sb * 1024 + tt * 128:
                                                      sb * 1024 + tt * 128 + 128],
                                      S[f"ysb{e % 2}"][:, sb * 1024 + cg * 512:
                                                       sb * 1024 + (cg + 1) * 512],
                                      start=(sb == 0), stop=(sb == NST - 1),
                                      skip_group_check=True),
                                  mark=(f"te_sc{gi}_{e}_{tt}_{cg}"
                                        if sb == NST - 1 else None),
                                  sem="ste")

    def body_vector(v):
        em.cur_engine = v

        def ts(out, in0, s1, s2, o1, o2=None):
            if o2 is None:
                return v.tensor_scalar(out, in0, s1, None, o1)
            return v.tensor_scalar(out, in0, s1, s2, o1, o2)

        em.op(lambda: v.memset(S["xsq"][:], 0.0), mark="ve_zrow", sem="sve")
        em.op(lambda: v.memset(S["ones"][:], 1.0), mark="ve_ones", sem="sve")

        for b in range(B_LOC):
            for cc in range(DCH):
                em.wait(f"wd_winc{b}_{cc}")
                if b == 0 and cc == 0:
                    em.wait("wd_k1")
                if b == 1 and cc == 0:
                    em.wait("te_cv0_1_31")
                em.op(lambda b=b, cc=cc: ts(
                    S["wp"][:, cc * PROJ:(cc + 1) * PROJ],
                    S[f"tmp16_{cc % 2}"][:], S["gcol"][:, cc:cc + 1], None, ALU.mult),
                    mark=f"ve_wp{b}_{cc}", sem="sve")
            if b == 0:
                for ng in range(6):
                    em.wait(f"te_bias{ng}")
                    em.op(lambda ng=ng: v.tensor_copy(S["srtmp"][0:1, :], PB[ng][0:1, :]),
                          mark=f"ve_srow_bias{ng}", sem="sve")
                    if ng < 5:
                        em.wait(f"sst_srow_r1_{ng}")
                for ng in range(6):
                    em.wait(f"te_sp{ng}")
                    em.op(lambda ng=ng: v.tensor_copy(
                        S["srow"][0:1, ng * 512:(ng + 1) * 512], PB[ng][0:1, :]),
                        mark=f"ve_srow_sp{ng}", sem="sve")

            for j in range(NT):
                q = b * NT + j
                X = S[f"x_in{q % 2}"]
                if b == 1 and j == 0:
                    em.wait("sst_accw14")
                    em.wait("sst_accw15")
                    em.wait("te_w0_15_1")
                em.wait(f"sx_x{q}")
                em.op(lambda X=X: v.reduce_sum(S["scr"][:, 0:1], X[:], axis=AX.X))
                em.op(lambda: v.drain())
                em.op(lambda: ts(S["scr"][:, 1:2], S["scr"][:, 0:1], 1.0 / D, None,
                                 ALU.mult))
                em.op(lambda: v.drain())
                em.op(lambda: ts(S["scr"][:, 2:3], S["scr"][:, 1:2], S["scr"][:, 1:2],
                                 None, ALU.mult))
                em.op(lambda: v.drain())
                em.wait(f"sc_sq1_{q}")
                em.op(lambda: v.scalar_tensor_tensor(
                    S["scr"][:, 3:4], S["scr"][:, 4:5], 1.0 / D, S["scr"][:, 2:3],
                    ALU.mult, ALU.subtract))
                em.op(lambda: v.drain())
                em.op(lambda: ts(S["scr"][:, 5:6], S["scr"][:, 3:4], EPS, None, ALU.add),
                      mark=f"ve_var1_{q}", sem="sve")
                em.wait(f"sc_ln1_{q}")
                em.op(lambda: ts(S["scr"][:, 7:8], S["scr"][:, 6:7], -0.5, None, ALU.mult),
                      mark=f"ve_lnh1_{q}", sem="sve")
                em.wait(f"sc_r1_{q}")
                if q >= 2:
                    em.wait(f"sc_sq2_{q - 2}")
                    em.wait(f"gp_acc{q - 2}")
                em.op(lambda X=X, q=q: ts(
                    S[f"xn32{q % 2}"][:], X[:], S["scr"][:, 1:2], S["scr"][:, 8:9],
                    ALU.subtract, ALU.mult), mark=f"ve_xn{q}", sem="sve")
                if q >= 2:
                    em.wait(f"te_tr{q - 2}")
                    em.wait(f"sst_xnt{q - 2}")
                em.op(lambda X=X, q=q: ts(
                    S[f"xnbf{q % 2}"][:], X[:], S["scr"][:, 1:2], S["scr"][:, 8:9],
                    ALU.subtract, ALU.mult), mark=f"ve_xnbf{q}", sem="sve")
                em.op(lambda: v.drain())
                if q >= 2:
                    em.wait(f"sst_acc{q - 2}")
                em.op(lambda X=X, q=q: v.tensor_add(
                    S[f"accp{q % 2}"][:], X[:], S[f"xn32{q % 2}"][:]),
                    mark=f"gp_acc{q}", sem="sve")
                em.op(lambda q=q: v.reduce_sum(S["scr"][:, 9:10],
                                               S[f"xn32{q % 2}"][:], axis=AX.X))
                em.op(lambda: v.drain())
                em.op(lambda: ts(S["scr"][:, 10:11], S["scr"][:, 9:10], 1.0 / D, None,
                                 ALU.mult))
                em.op(lambda: v.drain())
                em.op(lambda: ts(S["scr"][:, 11:12], S["scr"][:, 10:11], S["scr"][:, 10:11],
                                 None, ALU.mult))
                em.op(lambda: v.drain())
                em.wait(f"sc_sq2_{q}")
                em.op(lambda: v.scalar_tensor_tensor(
                    S["scr"][:, 12:13], S["scr"][:, 13:14], 1.0 / D, S["scr"][:, 11:12],
                    ALU.mult, ALU.subtract))
                em.op(lambda: v.drain())
                em.op(lambda: ts(S["scr"][:, 14:15], S["scr"][:, 12:13], EPS, None, ALU.add),
                      mark=f"ve_var2_{q}", sem="sve")
                em.wait(f"sc_ln2_{q}")
                em.op(lambda: ts(S["scr"][:, 16:17], S["scr"][:, 15:16], -0.5, None,
                                 ALU.mult), mark=f"ve_lnh2_{q}", sem="sve")
                em.wait(f"sc_r2_{q}")
                em.op(lambda: v.reciprocal(S["scr"][:, 18:19], S["scr"][:, 17:18]))
                em.op(lambda: v.drain())
                em.op(lambda: ts(S["scr"][:, 19:20], S["scr"][:, 10:11], -1.0, None,
                                 ALU.mult))
                em.op(lambda: v.drain())
                if q >= 1:
                    em.wait(f"te_st{q - 1}")
                em.op(lambda: v.tensor_copy(S["statp"][:, 0:1], S["scr"][:, 19:20]))
                em.op(lambda: v.tensor_copy(S["statp"][:, 1:2], S["scr"][:, 18:19]))
                em.op(lambda: v.drain(), mark=f"ve_stat{q}", sem="sve")
                for ng in range(6):
                    em.wait(f"te_proj{b}_{j}_{ng}")
                    if ng < 2:
                        if q >= 2:
                            em.wait(f"te_vrev{q - 2}_{ng}")
                        dst = S[f"vtmp{q % 2}"][:, ng * 512:(ng + 1) * 512]
                    elif ng < 4:
                        dst = S["x1tm"][:, j * 1024 + (ng - 2) * 512:
                                        j * 1024 + (ng - 1) * 512]
                    else:
                        dst = S["x2tm"][:, j * 1024 + (ng - 4) * 512:
                                        j * 1024 + (ng - 3) * 512]
                    em.op(lambda dst=dst, ng=ng: ts(
                        dst, PB[ng % 2][:], S["scr"][:, 17:18], None, ALU.mult),
                        mark=f"ve_pevac{q}_{ng}", sem="sve")
                em.wait(f"te_rt{q}")
                em.op(lambda: v.max(out=S["scr2"][:, 0:8], in_=PB[6][:, 0:8]))
                em.op(lambda: v.drain())
                em.op(lambda: ts(S["scr2"][:, 8:16], PB[6][:, 0:8], S["scr2"][:, 0:1],
                                 None, ALU.subtract), mark=f"ve_lsub{q}", sem="sve")
                em.op(lambda: ts(S["scr2"][:, 16:24], PB[6][:, 0:8], S["scr2"][:, 1:2],
                                 None, ALU.is_ge))
                em.op(lambda: v.drain())
                em.wait(f"sc_exp{q}")
                em.op(lambda: v.reduce_sum(S["scr2"][:, 28:29], S["rt8"][:], axis=AX.X))
                em.op(lambda: v.drain())
                em.op(lambda: v.tensor_mul(S["rt8"][:], S["rt8"][:], S["scr2"][:, 16:24]))
                em.op(lambda: v.drain())
                em.op(lambda: v.reduce_sum(S["scr2"][:, 24:25], S["rt8"][:], axis=AX.X))
                em.op(lambda: v.drain())
                em.op(lambda: v.scalar_tensor_tensor(
                    S["scr2"][:, 25:26], S["scr2"][:, 28:29], 1e-9, S["scr2"][:, 24:25],
                    ALU.mult, ALU.add))
                em.op(lambda: v.drain())
                em.op(lambda: v.reciprocal(S["scr2"][:, 27:28], S["scr2"][:, 25:26]))
                em.op(lambda: v.drain())
                em.op(lambda q=q: ts(S["coef"][:, q * 8:(q + 1) * 8], S["rt8"][:],
                                     S["scr2"][:, 27:28], None, ALU.mult),
                      mark=f"ve_coef{q}", sem="sve")

            def gate_evacs(cv):
                gate = S["x1tm"] if cv == 0 else S["x2tm"]
                ydst = S["y1tm"] if cv == 0 else S["y2tm"]
                for cg in range(32):
                    em.wait(f"te_cv{b}_{cv}_{cg}")
                    if cv == 0 and cg == 0:
                        em.wait(f"sst_acc{qlast(b) - 1}")
                        em.wait(f"sst_acc{qlast(b)}")
                        em.wait(f"sst_xnt{qlast(b) - 1}")
                        em.wait(f"sst_xnt{qlast(b)}")
                    em.op(lambda cg=cg, gate=gate, ydst=ydst, bank=PB[cg % 2]:
                          v.tensor_mul(
                              rearr(ydst)[:, cg * 32:(cg + 1) * 32, :],
                              bank[:].rearrange("p (c j) -> p c j", c=32),
                              rearr(gate)[:, cg * 32:(cg + 1) * 32, :]),
                          mark=f"ve_g{b}_{cv}_{cg}", sem="sve")
            gate_evacs(0)
            for h in range(32):
                em.wait(f"te_r1_{b}_{h}")
                em.op(lambda h=h: v.tensor_copy(
                    S["y1rev"][:, h * 512:(h + 1) * 512], PB[2 + h % 2][:]),
                    mark=f"ve_r1ev_{b}_{h}", sem="sve")
            gate_evacs(1)
            for j in range(NT):
                q = b * NT + j
                em.wait(f"te_y2tr{b}_{j}")
                if j >= 2:
                    em.wait(f"te_w{b}_{j - 2}_1")
                em.op(lambda j=j: v.tensor_copy(S[f"y2t{j % 2}"][:], pb7[:]),
                      mark=f"ve_y2tev{b}_{j}", sem="sve")
                for cg in range(2):
                    em.wait(f"te_w{b}_{j}_{cg}")
                    em.wait(f"sx_accin{q}")
                    if j == 0:
                        em.wait(f"sc_exp{qlast(b)}")
                        if b == 1:
                            em.wait("sst_accw14")
                            em.wait("sst_accw15")
                    if j >= 2:
                        em.wait(f"sst_accw{q - 2}")
                    em.op(lambda j=j, cg=cg: v.scalar_tensor_tensor(
                        S[f"accout{j % 2}"][:, cg * 512:(cg + 1) * 512],
                        PB[4 + cg][:], 1.0,
                        S[f"accin{j % 2}"][:, cg * 512:(cg + 1) * 512],
                        ALU.mult, ALU.add),
                        mark=f"ve_wev{b}_{j}_{cg}", sem="sve")

        em.op(lambda: v.drain(), mark="ve_y1_done", sem="sve")

        em.wait("sst_acc30")
        em.wait("sst_acc31")
        em.wait("sst_accw30")
        em.wait("sst_accw31")
        em.wait("te_b1t")
        em.op(lambda: v.tensor_copy(S["b1call"][:], pb7[:, 0:256]),
              mark="ve_b1c", sem="sve")
        em.op(lambda: v.tensor_copy(S["coefbf"][:], S["coef"][:]),
              mark="ve_cbf", sem="sve")

        for gi in range(NGRP):
            CS = S["coef"][:, gi * 64:(gi + 1) * 64]

            def gt_build(e, gi=gi):
                ge = gi * NE + e
                if ge >= 2:
                    p2g, p2e = divmod(ge - 2, NE)
                    em.wait(f"te_gstr{p2g}_{p2e}_2")
                for tt in range(8):
                    em.op(lambda e=e, tt=tt: v.tensor_scalar(
                        S[f"gt{e % 2}"][:, tt * 384:(tt + 1) * 384],
                        S["iot"][:], S["posd1"][:, tt * 8 + e: tt * 8 + e + 1],
                        None, ALU.is_equal),
                        mark=(f"ve_gt{gi}_{e}" if tt == 7 else None), sem="sve")

            if gi > 0:
                em.wait(f"te_pos1{gi - 1}")
            em.op(lambda CS=CS: v.tensor_scalar(S["maskm"][:], CS, 0.0, None,
                                                ALU.not_equal),
                  mark=f"ve_mask{gi}", sem="sve")
            em.wait(f"te_cnt{gi}")
            em.op(lambda: v.tensor_copy(S["row0"][0:1, :], PB[6][0:1, 0:64]),
                  mark=f"ve_cnt{gi}", sem="sve")
            em.op(lambda: v.drain())
            em.op(lambda: v.memset(S["row1"][0:1, 0:8], 0.0))
            em.op(lambda: v.tensor_copy(S["row1"][0:1, 8:64], S["row0"][0:1, 0:56]))
            em.op(lambda: v.drain())
            em.op(lambda: v.tensor_copy(S["row2"][0:1, :], S["row1"][0:1, :]))
            em.op(lambda: v.drain())
            em.op(lambda: v.tensor_add(S["row2"][0:1, 8:64], S["row1"][0:1, 8:64],
                                       S["row1"][0:1, 0:56]))
            em.op(lambda: v.drain())
            em.op(lambda: v.tensor_copy(S["row3"][0:1, :], S["row2"][0:1, :]))
            em.op(lambda: v.drain())
            em.op(lambda: v.tensor_add(S["row3"][0:1, 16:64], S["row2"][0:1, 16:64],
                                       S["row2"][0:1, 0:48]))
            em.op(lambda: v.drain())
            em.op(lambda: v.tensor_add(S["row3"][0:1, 32:64], S["row3"][0:1, 32:64],
                                       S["row3"][0:1, 0:32]))
            em.op(lambda: v.drain(), mark=f"ve_base{gi}", sem="sve")
            em.wait(f"te_pos1{gi}")
            em.op(lambda: v.tensor_copy(S["posd0"][:], PB[6][:, 0:64]))
            em.op(lambda: v.drain())
            em.op(lambda: v.tensor_scalar(S["tt0"][:], S["maskm"][:], -1.0e9, 1.0e9,
                                          ALU.mult, ALU.add))
            em.op(lambda: v.tensor_scalar(S["tt1"][:], S["posd0"][:], float(CAP),
                                          1.0e9, ALU.is_ge, ALU.mult))
            em.op(lambda: v.drain())
            em.op(lambda: v.tensor_add(S["tt2"][:], S["tt0"][:], S["tt1"][:]))
            em.op(lambda: v.drain())
            em.op(lambda: v.tensor_add(S["posd1"][:], S["posd0"][:], S["tt2"][:]))
            em.op(lambda: v.drain(), mark=f"ve_pd{gi}", sem="sve")
            gt_build(0)

            for e in range(NE):
                ge = gi * NE + e
                pgi, pe = divmod(ge - 1, NE)
                for sb in range(NST):
                    em.wait(f"te_gmm{gi}_{e}_{sb}")
                    if ge == 0 and sb == 0:
                        em.wait("te_cv1_1_31")
                    if sb == 2:
                        em.wait(f"te_xgtr{gi}_{e}_0")
                    elif ge > 0:
                        em.wait(f"te_xgtr{pgi}_{pe}_{2 - sb}")
                    bk0, bk1 = GBANK(sb)
                    em.op(lambda sb=sb, bk0=bk0: v.tensor_copy(
                        S[f"xgsb{sb % 2}"][:, 0:512], bk0[:]))
                    em.op(lambda sb=sb, bk1=bk1: v.tensor_copy(
                        S[f"xgsb{sb % 2}"][:, 512:1024], bk1[:]))
                    em.op(lambda e=e, sb=sb: v.tensor_copy(
                        S[f"gcoefc{e % 2}"][:, sb:sb + 1], PB[6][:, 64 + sb:65 + sb]),
                        mark=f"ve_xgev{gi}_{e}_{sb}", sem="sve")
                for sb in range(NST):
                    em.wait(f"te_xgtr{gi}_{e}_{sb}")
                    if sb == 0:
                        if ge == 0:
                            em.wait("te_proj1_15_5")
                        if ge >= 2:
                            p2g, p2e = divmod(ge - 2, NE)
                            em.wait(f"te_a{p2g}_{p2e}_31")
                    em.op(lambda e=e, sb=sb: v.tensor_copy(
                        S[f"xgt{e % 2}"][:].rearrange("p (c s) -> p c s", c=DCH)
                        [:, :, sb * 128:(sb + 1) * 128],
                        pb7[:].rearrange("p (c s) -> p c s", c=DCH)),
                        mark=f"ve_xgtev{gi}_{e}_{sb}", sem="sve")
                if e + 1 < NE:
                    gt_build(e + 1)
                for phi in range(32):
                    em.wait(f"te_a{gi}_{e}_{phi}")
                    if ge == 0 and phi == 0:
                        em.wait("te_y2tr1_15")
                    if ge > 0:
                        em.wait(f"te_bphi{pgi}_{pe}_{phi}" if phi < 31
                                else f"te_b{pgi}_{pe}_5")
                    em.op(lambda gi=gi, e=e, phi=phi: v.tensor_scalar(
                        S["hT"][:, phi * CAP:(phi + 1) * CAP],
                        PB[phi % 2][:, 0:CAP],
                        S["b1call"][:, e * 32 + phi: e * 32 + phi + 1], 0.0,
                        ALU.add, ALU.max),
                        mark=f"ve_h{gi}_{e}_{phi}", sem="sve")
                for sb in range(NST):
                    em.wait(f"te_gstr{gi}_{e}_{sb}")
                    if sb == 0 and ge >= 2:
                        p2g, p2e = divmod(ge - 2, NE)
                        em.wait(f"te_sc{p2g}_{p2e}_7_1")
                    em.op(lambda e=e, sb=sb: v.tensor_copy(
                        S[f"gs{e % 2}"][:, sb * 1024:(sb + 1) * 1024], pb7[:]),
                        mark=f"ve_gsev{gi}_{e}_{sb}", sem="sve")
                first_ci = True
                for ci in (0, 2, 4, 1, 3, 5):
                    st, cg = ci // 2, ci % 2
                    em.wait(f"te_b{gi}_{e}_{ci}")
                    if first_ci and ge >= 2:
                        p2g, p2e = divmod(ge - 2, NE)
                        em.wait(f"te_sc{p2g}_{p2e}_7_1")
                    first_ci = False
                    em.op(lambda e=e, st=st, cg=cg, ci=ci: v.tensor_scalar(
                        S[f"ysb{e % 2}"][:, st * 1024 + cg * 512:
                                         st * 1024 + (cg + 1) * 512],
                        (pb7f if BCH[ci] == 7 else PB[BCH[ci]])[:],
                        S[f"gcoefc{e % 2}"][:, st:st + 1], None, ALU.mult),
                        mark=f"ve_ysb{gi}_{e}_{ci}", sem="sve")
                for tt in range(8):
                    for cg in range(2):
                        em.wait(f"te_sc{gi}_{e}_{tt}_{cg}")
                        if e == 0 and tt == 0 and cg == 0:
                            em.wait(f"wd_yg{gi}")
                        em.op(lambda tt=tt, cg=cg: v.tensor_add(
                            S["ygrp"][:, tt * 1024 + cg * 512:
                                      tt * 1024 + (cg + 1) * 512],
                            S["ygrp"][:, tt * 1024 + cg * 512:
                                      tt * 1024 + (cg + 1) * 512],
                            PB[1 + 2 * (tt % 2) + cg][:]),
                            mark=f"ve_ysum{gi}_{e}_{tt}_{cg}", sem="sve")

    def body_scalar(s):
        em.cur_engine = s
        for b in range(B_LOC):
            for j in range(NT):
                q = b * NT + j
                X = S[f"x_in{q % 2}"]
                em.wait(f"sx_x{q}")
                if q == 0:
                    em.wait("wd_k1")
                em.op(lambda X=X: s.activation(S["xsq"][:], X[:], AF.Square,
                                               accum_out=S["scr"][:, 4:5]),
                      mark=f"sc_sq1_{q}", sem="ssc")
                em.wait(f"ve_var1_{q}")
                em.op(lambda: s.activation(S["scr"][:, 6:7], S["scr"][:, 5:6], AF.Ln),
                      mark=f"sc_ln1_{q}", sem="ssc")
                em.wait(f"ve_lnh1_{q}")
                em.op(lambda: s.activation(S["scr"][:, 8:9], S["scr"][:, 7:8], AF.Exp),
                      mark=f"sc_r1_{q}", sem="ssc")
                if j >= 1:
                    for h in range(2):
                        em.wait(f"te_vrev{q - 1}_{h}")
                        pj = (q - 1) % NT
                        em.op(lambda pj=pj, h=h: s.activation(
                            S["vrev"][:, pj * 1024 + h * 512: pj * 1024 + (h + 1) * 512],
                            PB[2 + h][:], AF.Copy),
                            mark=f"sc_vrev{q - 1}_{h}", sem="ssc")
                em.wait(f"ve_xn{q}")
                em.op(lambda q=q: s.activation(S["xsq"][:], S[f"xn32{q % 2}"][:],
                                               AF.Square, accum_out=S["scr"][:, 13:14]),
                      mark=f"sc_sq2_{q}", sem="ssc")
                em.wait(f"ve_var2_{q}")
                em.op(lambda: s.activation(S["scr"][:, 15:16], S["scr"][:, 14:15], AF.Ln),
                      mark=f"sc_ln2_{q}", sem="ssc")
                em.wait(f"ve_lnh2_{q}")
                em.op(lambda: s.activation(S["scr"][:, 17:18], S["scr"][:, 16:17], AF.Exp),
                      mark=f"sc_r2_{q}", sem="ssc")
                em.wait(f"te_tr{q}")
                if q >= 2:
                    pq = q - 2
                    em.wait(f"te_proj{pq // NT}_{pq % NT}_5")
                    em.wait(f"te_rt{pq}")
                em.op(lambda q=q: s.activation(S[f"xnt{q % 2}"][:], pb7[:], AF.Copy),
                      mark=f"sc_xnt{q}", sem="ssc")
                em.wait(f"te_st{q}")
                if q >= 1:
                    pq = q - 1
                    em.wait(f"te_proj{pq // NT}_{pq % NT}_5")
                em.op(lambda: s.activation(S["strows"][0:2, :], pb7[0:2, 0:128],
                                           AF.Copy),
                      mark=f"sc_srt{q}", sem="ssc")
                em.wait(f"ve_lsub{q}")
                em.op(lambda: s.activation(S["rt8"][:], S["scr2"][:, 8:16], AF.Exp),
                      mark=f"sc_exp{q}", sem="ssc")
                if j == NT - 1:
                    for h in range(2):
                        em.wait(f"te_vrev{q}_{h}")
                        em.op(lambda j=j, h=h: s.activation(
                            S["vrev"][:, j * 1024 + h * 512: j * 1024 + (h + 1) * 512],
                            PB[2 + h][:], AF.Copy),
                            mark=f"sc_vrev{q}_{h}", sem="ssc")
            for cv in range(2):
                for cg in range(32):
                    for ch in (0,):
                        prev = cg * 2 + ch - 4
                        if prev >= 0:
                            em.wait(f"te_cv{b}_{cv}_{prev // 2}")
                        elif cv == 1:
                            em.wait(f"te_cv{b}_0_{(64 + prev) // 2}")
                        else:
                            em.wait("wd_k1")
                            em.wait(f"te_proj{b}_15_5")
                        em.wait(f"gp_txe{cv}")
                        em.op(lambda cv=cv, ch=ch, cg=cg,
                              slot=(cg * 2 + ch) % 4: s.dma_start(
                            out=S[f"toep{slot}"][:],
                            in_=AP(dtoep.tensor,
                                   ((cv * 2 + ch) * 32 + cg) * 128 * 4096,
                                   [[4096, 128], [1, 4096]])),
                            mark=f"wd_tp{b}_{cv}_{cg}_{ch}",
                            sem=f"stp{(cg * 2 + ch) % 4}", dma=True)
            em.wait(f"te_r1_{b}_31")
            if b == 0:
                em.wait("cv_wo")
            em.op(lambda: s.dma_start(
                out=S["wo"][:].rearrange("p (c n) -> p c n", c=DCH),
                in_=AP(dwob.tensor, 0, [[D, 128], [128 * D, DCH], [1, D]])),
                mark=f"wd_wo{b}", sem="swo", dma=True)


        em.wait("ve_y1_done")
        em.wait("gp_acc31")
        if _DEBUG:
            em.wait("sst_dbgy2")
        for gi in range(NGRP):
            for e in range(NE):
                if gi == 0:
                    em.wait(f"cv_w1_{e}")
                for pc in range(0, 16, 2):
                    if not (gi == 0 and e == 0 and pc < 4):
                        if pc >= 4:
                            em.wait(f"te_a{gi}_{e}_{pc * 2 - 7}")
                        else:
                            pe, pg = (e - 1, gi) if e > 0 else (NE - 1, gi - 1)
                            em.wait(f"te_a{pg}_{pe}_{pc * 2 + 25}")
                    em.op(lambda e=e, pc=pc: s.dma_start(
                        out=S[f"w1c{pc % 4}"][:].rearrange("p (c f) -> p c f", c=DCH),
                        in_=AP(dw1b.tensor, e * D * FF + pc * 256,
                               [[FF, 128], [128 * FF, DCH], [1, 256]])),
                        mark=f"wd_w1_{gi}_{e}_{pc}", sem=f"sw1{pc % 4}", dma=True)

    bodies = [body_gpsimd, body_sync, body_tensor, body_vector, body_scalar]

    class _Mock:
        def wait_ge(self, *a, **k):
            pass

    em.start_pass(1)
    em.cur_engine = _Mock()
    mock = em.cur_engine
    for f in bodies:
        em.cur_engine = mock
        f(None)

    em.start_pass(2)
    from contextlib import ExitStack
    with ExitStack() as ctx:
        for sn in SEMNAMES:
            em.sems[sn] = ctx.enter_context(nc.semaphore(sn))
        block = ctx.enter_context(nc.Block())
        block.gpsimd(lambda g: body_gpsimd(g))
        block.sync(lambda sy: body_sync(sy))
        block.tensor(lambda t: body_tensor(t))
        block.vector(lambda v: body_vector(v))
        block.scalar(lambda s: body_scalar(s))
    return nc


_NC = None


def kernel(L0_attn_x, L0_attn_W_in, L0_attn_b_in, L0_attn_k0, L0_attn_k1,
           L0_attn_gamma, L0_attn_beta, L0_attn_W_out, L0_attn_b_out,
           L0_ffn_router_w, L0_ffn_w1, L0_ffn_b1, L0_ffn_w2, L0_ffn_b2):
    global _NC
    x = np.ascontiguousarray(np.asarray(L0_attn_x, dtype=np.float32))
    w_in = np.ascontiguousarray(np.asarray(L0_attn_W_in, dtype=np.float32))
    b_in = np.ascontiguousarray(np.asarray(L0_attn_b_in, dtype=np.float32))
    k0 = np.ascontiguousarray(np.asarray(L0_attn_k0, dtype=np.float32)[0])
    k1 = np.ascontiguousarray(np.asarray(L0_attn_k1, dtype=np.float32)[0])
    gamma = np.ascontiguousarray(np.asarray(L0_attn_gamma, dtype=np.float32))
    beta = np.ascontiguousarray(np.asarray(L0_attn_beta, dtype=np.float32))
    w_out = np.ascontiguousarray(np.asarray(L0_attn_W_out, dtype=np.float32))
    b_out = np.ascontiguousarray(np.asarray(L0_attn_b_out, dtype=np.float32))
    rw = np.ascontiguousarray(np.asarray(L0_ffn_router_w, dtype=np.float32).T)
    w1t = np.ascontiguousarray(np.asarray(L0_ffn_w1, dtype=np.float32).transpose(0, 2, 1))
    b1 = np.ascontiguousarray(np.asarray(L0_ffn_b1, dtype=np.float32))
    w2t = np.ascontiguousarray(np.asarray(L0_ffn_w2, dtype=np.float32).transpose(0, 2, 1))
    b2 = np.ascontiguousarray(np.asarray(L0_ffn_b2, dtype=np.float32))

    if _NC is None:
        _NC = _build_nc()

    in_maps = []
    for c in range(8):
        xs = np.ascontiguousarray(x[2 * c: 2 * c + 2].reshape(TTOT, D))
        in_maps.append(dict(x=xs, w_in=w_in, b_in=b_in, k0=k0, k1=k1,
                            gamma=gamma, beta=beta, w_out=w_out, b_out=b_out,
                            rw=rw, w1t=w1t, b1=b1, w2t=w2t, b2=b2))
    res = run_bass_kernel_spmd(_NC, in_maps, core_ids=list(range(8)))
    outs = [np.asarray(res.results[c]["out"]).reshape(2, T, D) for c in range(8)]
    return np.concatenate(outs, axis=0)
